# version 10
# baseline (speedup 1.0000x reference)
import numpy as np
import ml_dtypes
import concourse.bass as bass
import concourse.mybir as mybir
from concourse.bass_utils import run_bass_kernel_spmd

F32 = mybir.dt.float32
BF16 = mybir.dt.bfloat16
I32 = mybir.dt.int32
AX = mybir.AxisListType.X
OP = mybir.AluOpType
AF = mybir.ActivationFunctionType

S = 2048
D = 1024
NT = 16
DC = 8
NPAIR = 8
CH = 4096
NCB = 32
NTB = 8
TBW = 256
MAGIC = 12582912.0
STAGE = "full"
DBG_NOVBIAS = False
DBG_TB = 0


class Emitter:
    ENGINES = ("sync", "vector", "scalar", "tensor", "gpsimd")
    AUTO = {"vector": "sv", "scalar": "sa", "gpsimd": "sg"}

    def __init__(self):
        self.events = {e: [] for e in self.ENGINES}
        self.counts = {}
        self.handles = {}
        self.last_wait = {e: {} for e in self.ENGINES}

    def add_sem(self, name, handle):
        self.handles[name] = handle
        self.counts[name] = 0

    def emit(self, eng, fn=None, waits=(), inc=None):
        auto = self.AUTO.get(eng)
        if fn is not None and auto is not None:
            waits = list(waits) + [(auto, self.counts[auto])]
            if inc is None:
                inc = (auto, 1)
        evs = []
        for s, v in waits:
            if v > 0 and self.last_wait[eng].get(s, -1) < v:
                evs.append((s, v))
                self.last_wait[eng][s] = v
        self.events[eng].append((evs, fn, inc))
        if inc is not None:
            self.counts[inc[0]] += inc[1]

    def now(self, sem):
        return self.counts[sem]

    def replay(self, block):
        handles = self.handles
        for engname in self.ENGINES:
            evlist = self.events[engname]

            def body(eng, evlist=evlist):
                for waits, fn, inc in evlist:
                    for (s, v) in waits:
                        eng.wait_ge(handles[s], v)
                    if fn is not None:
                        inst = fn(eng)
                        if inc is not None:
                            inst.then_inc(handles[inc[0]], inc[1])
            getattr(block, engname)(body)


def build_program():
    nc = bass.Bass("TRN2", target_bir_lowering=False,
                   detect_race_conditions=False)

    xin = nc.dram_tensor("xin", [S, D], F32, kind="ExternalInput").ap()
    wqkv = nc.dram_tensor("wqkv", [NPAIR, 128, 3072], F32, kind="ExternalInput").ap()
    woutT = nc.dram_tensor("woutT", [D, D], F32, kind="ExternalInput").ap()
    w1T = nc.dram_tensor("w1T", [D, CH], F32, kind="ExternalInput").ap()
    w2T = nc.dram_tensor("w2T", [CH, D], F32, kind="ExternalInput").ap()
    qkcols = nc.dram_tensor("qkcols", [128, 16], F32, kind="ExternalInput").ap()
    bvrow = nc.dram_tensor("bvrow", [1, D], F32, kind="ExternalInput").ap()
    borow = nc.dram_tensor("borow", [1, D], F32, kind="ExternalInput").ap()
    cwcols = nc.dram_tensor("cwcols", [128, 96], F32, kind="ExternalInput").ap()
    cbcols = nc.dram_tensor("cbcols", [128, 32], F32, kind="ExternalInput").ap()
    cid32 = nc.dram_tensor("cid32", [128, 128], F32, kind="ExternalInput").ap()
    cidbf = nc.dram_tensor("cidbf", [128, 128], BF16, kind="ExternalInput").ap()
    cdiag = nc.dram_tensor("cdiag", [128, 128], BF16, kind="ExternalInput").ap()
    ccols = nc.dram_tensor("ccols", [128, 8], F32, kind="ExternalInput").ap()
    conerow = nc.dram_tensor("conerow", [1, 128], F32, kind="ExternalInput").ap()
    out = nc.dram_tensor("out", [S, D], F32, kind="ExternalOutput").ap()
    dbg = nc.dram_tensor("dbg", [128, 8192], BF16, kind="ExternalOutput").ap()
    dbgf = nc.dram_tensor("dbgf", [128, 512], F32, kind="ExternalOutput").ap()
    dbgh = nc.dram_tensor("dbgh", [128, 8192], F32, kind="ExternalOutput").ap()
    m1h = nc.dram_tensor("m1h", [16, 128, 8, 128], BF16, kind="Internal").ap()
    m2h = nc.dram_tensor("m2h", [16, 128, D], BF16, kind="Internal").ap()

    from contextlib import ExitStack
    ctx = ExitStack()
    e = ctx.enter_context

    X = e(nc.sbuf_tensor("X", [128, NT * D], F32))
    SCR = e(nc.sbuf_tensor("SCR", [128, 4096], F32))
    WQ = e(nc.sbuf_tensor("WQ", [128, 1024], F32))
    COL = e(nc.sbuf_tensor("COL", [128, 160], F32))
    QKB = e(nc.sbuf_tensor("QKB", [128, 16], F32))
    CW = e(nc.sbuf_tensor("CW", [128, 96], F32))
    CBc = e(nc.sbuf_tensor("CBc", [128, 32], F32))
    CC = e(nc.sbuf_tensor("CC", [128, 8], F32))
    ID32 = e(nc.sbuf_tensor("ID32", [128, 128], F32))
    ONER = e(nc.sbuf_tensor("ONER", [1, 128], F32))
    BROW = e(nc.sbuf_tensor("BROW", [1, D], F32))
    S2R = e(nc.sbuf_tensor("S2R", [1, S + 2], F32))
    RMX = e(nc.sbuf_tensor("RMX", [128, TBW], F32))
    B1 = e(nc.sbuf_tensor("B1", [128, 16384], BF16))
    B2 = e(nc.sbuf_tensor("B2", [128, 16384], BF16))
    PAIR = e(nc.sbuf_tensor("PAIR", [128, 11264], BF16))
    IDBF = e(nc.sbuf_tensor("IDBF", [128, 128], BF16))
    DIAG = e(nc.sbuf_tensor("DIAG", [128, 128], BF16))
    ETB = e(nc.sbuf_tensor("ETB", [128, 4 * 128], BF16))
    N1B = e(nc.sbuf_tensor("N1B", [128, 4 * 2064], BF16))
    N1T = e(nc.sbuf_tensor("N1T", [128, 1024], BF16))
    WI = e(nc.sbuf_tensor("WI", [128, 512], I32))
    PS = [e(nc.psum_tensor(f"PS{i}", [128, 512], F32)) for i in range(4)]
    PO = [e(nc.psum_tensor(f"PO{i}", [128, 512], F32)) for i in range(2)]
    PT = e(nc.psum_tensor("PT", [128, 1024], BF16))
    PF = e(nc.psum_tensor("PF", [128, 512], F32))

    PQ, PK, PV, PW = 0, 2048, 4096, 6176
    PN2, PM1S, PM2S = 0, 8192, 10240
    PWO, POF = 0, 8192

    C_SUM, C_SQ, C_M, C_R, C_RM = 0, 16, 32, 48, 64
    C_U, C_U23 = 80, 81
    C_C1, C_T1, C_NT1 = 82, 83, 84
    C_C2, C_T2, C_NT2 = 85, 86, 87
    C_W = 96
    C_RD = 112
    C_S1 = 120
    C_Y = 136
    C_X = 144

    S_LN = 0
    S_WST = 1024
    S_WOS = 1024
    S_QP = 1024
    S_HSC = 1024
    S_UB = 1560
    S_BC = 2100

    RMF = SCR[0:64, 0:TBW]
    X2RB = SCR[:, 2700:2700 + D]

    em = Emitter()
    E = em.emit
    sem_names = ("lc", "lx", "lw", "lq", "lm", "lf", "lr", "ls",
                 "sv", "sa", "st", "sg", "so")
    for name in sem_names:
        em.add_sem(name, e(nc.semaphore(name)))

    for (dst, src) in ((QKB, qkcols), (CW, cwcols), (CBc, cbcols), (CC, ccols),
                       (ID32, cid32), (ONER, conerow), (BROW, bvrow),
                       (IDBF, cidbf), (DIAG, cdiag)):
        E("sync", lambda s, dst=dst, src=src: s.dma_start(out=dst[:], in_=src),
          inc=("lc", 16))
    lc_all = em.now("lc")
    for tt in range(NT):
        E("sync", lambda s, tt=tt: s.dma_start(
            out=X[:, tt * D:(tt + 1) * D], in_=xin[tt * 128:(tt + 1) * 128, :]),
          inc=("lx", 16))
    lx_all = em.now("lx")
    wst = SCR[:, S_WST:S_WST + 3072]
    E("sync", lambda s: s.dma_start(out=wst, in_=wqkv[0]), inc=("lq", 16))

    E("vector", lambda v: v.tensor_scalar(
        COL[:, C_U:C_U + 1], CC[:, 2:3], 0.5, -8388608.0, OP.add, OP.add),
      waits=[("lc", lc_all)], inc=("sv", 1))
    E("vector", lambda v: v.tensor_single_scalar(
        COL[:, C_U23:C_U23 + 1], COL[:, C_U:C_U + 1], MAGIC, OP.add),
      waits=[("sv", em.now("sv"))], inc=("sv", 1))
    E("vector", lambda v: v.memset(
        PAIR.ap()[:, PV + 64:PV + 16 * 130:130], 1.0), inc=("sv", 1))
    E("vector", lambda v: v.memset(
        PAIR.ap()[:, PV + 129:PV + 16 * 130:130], 1.0), inc=("sv", 1))
    E("vector", lambda v: v.memset(S2R[0:1, 0:1], 1.0), inc=("sv", 1))
    E("vector", lambda v: v.memset(S2R[0:1, S + 1:S + 2], 1.0), inc=("sv", 1))
    E("vector", lambda v: v.tensor_single_scalar(
        QKB[:, 0:16:2], QKB[:, 0:16:2], 0.125, OP.mult),
      waits=[("lc", lc_all)], inc=("sv", 1))
    sv_init = em.now("sv")

    def dump_and_stop(bfap, f32ap, waits):
        if bfap is not None:
            E("sync", lambda s: s.dma_start(out=dbg[:, 0:bfap.shape[1]], in_=bfap),
              waits=waits, inc=("so", 16))
        if f32ap is not None:
            E("sync", lambda s: s.dma_start(out=dbgf[:, 0:f32ap.shape[1]], in_=f32ap),
              waits=waits, inc=("so", 16))
        E("gpsimd", None, waits=[("so", em.now("so"))])
        with nc.Block() as block:
            em.replay(block)
        ctx.close()
        return nc

    def ln_stats(tt, waits):
        xt = X[:, tt * D:(tt + 1) * D]
        E("scalar", lambda a, xt=xt, tt=tt: a.activation(
            xt, xt, AF.Copy,
            accum_out=COL[:, C_SUM + tt:C_SUM + tt + 1]),
          waits=waits + [("sa", em.now("sa")), ("sv", em.now("sv"))],
          inc=("sa", 1))
        E("scalar", lambda a, tt=tt: a.activation(
            SCR[:, S_LN:S_LN + D], X[:, tt * D:(tt + 1) * D], AF.Square,
            accum_out=COL[:, C_SQ + tt:C_SQ + tt + 1]),
          inc=("sa", 1))
        return em.now("sa")

    def ln_cols(tt, sa_wait):
        mc = COL[:, C_M + tt:C_M + tt + 1]
        rc = COL[:, C_R + tt:C_R + tt + 1]
        vw = COL[:, C_W:C_W + 1]
        t2 = COL[:, C_W + 1:C_W + 2]
        E("vector", lambda v, mc=mc, tt=tt: v.tensor_single_scalar(
            mc, COL[:, C_SUM + tt:C_SUM + tt + 1], 1.0 / D, OP.mult),
          waits=[("sa", sa_wait), ("sv", em.now("sv"))], inc=("sv", 1))
        E("vector", lambda v, vw=vw, mc=mc: v.tensor_tensor(vw, mc, mc, OP.mult),
          inc=("sv", 1))
        E("vector", lambda v, vw=vw, tt=tt: v.scalar_tensor_tensor(
            vw, COL[:, C_SQ + tt:C_SQ + tt + 1], 1.0 / D, vw, OP.mult, OP.subtract),
          inc=("sv", 1))
        E("vector", lambda v, vw=vw: v.tensor_single_scalar(vw, vw, 1e-5, OP.add),
          inc=("sv", 1))
        sv_v = em.now("sv")
        E("scalar", lambda a, rc=rc, vw=vw: a.activation(rc, vw, AF.Ln),
          waits=[("sv", sv_v), ("sa", em.now("sa"))], inc=("sa", 1))
        E("scalar", lambda a, rc=rc: a.activation(rc, rc, AF.Exp, scale=-0.5),
          inc=("sa", 1))
        sa_z = em.now("sa")
        for _ in range(2):
            E("vector", lambda v, t2=t2, rc=rc: v.tensor_tensor(t2, rc, rc, OP.mult),
              waits=[("sa", sa_z), ("sv", em.now("sv"))], inc=("sv", 1))
            E("vector", lambda v, t2=t2, vw=vw: v.tensor_tensor(t2, t2, vw, OP.mult),
              inc=("sv", 1))
            E("vector", lambda v, t2=t2: v.tensor_scalar(
                t2, t2, -0.5, 1.5, OP.mult, OP.add), inc=("sv", 1))
            E("vector", lambda v, rc=rc, t2=t2: v.tensor_tensor(rc, rc, t2, OP.mult),
              inc=("sv", 1))
        return em.now("sv")

    pt_sv = [0]

    def transpose_to(src_ap, dst_ap, slot, waits=()):
        p = PT[:, slot * 128:(slot + 1) * 128]
        E("tensor", lambda t, p=p, src_ap=src_ap: t.transpose(p, src_ap, IDBF[:]),
          waits=list(waits) + [("sv", pt_sv[0])], inc=("st", 1))
        stv = em.now("st")
        E("vector", lambda v, p=p, dst_ap=dst_ap: v.tensor_copy(dst_ap, p),
          waits=[("st", stv)], inc=("sv", 1))
        pt_sv[0] = em.now("sv")
        return em.now("sv")

    sa_s = [0] * NT
    for tt in range(NT):
        sa_s[tt] = ln_stats(tt, [("lx", lx_all), ("lc", lc_all)])
    for tt in range(NT):
        ln_cols(tt, sa_s[tt])
    sv_ln1c = em.now("sv")
    for tt in range(NT):
        E("vector", lambda v, tt=tt: v.tensor_scalar(
            N1T[:, 0:D], X[:, tt * D:(tt + 1) * D],
            COL[:, C_M + tt:C_M + tt + 1], COL[:, C_R + tt:C_R + tt + 1],
            OP.subtract, OP.mult),
          waits=[("sv", sv_ln1c), ("st", em.now("st"))], inc=("sv", 1))
        sv_ap = em.now("sv")
        for dc in range(DC):
            transpose_to(N1T[:, dc * 128:(dc + 1) * 128],
                         B1[:, dc * 2048 + tt * 128:dc * 2048 + (tt + 1) * 128],
                         dc % 2, waits=[("sv", sv_ap)])
    sa_ln1fm = em.now("sa")
    sv_ln1fm = em.now("sv")

    if STAGE == "ln1":
        return dump_and_stop(B1[:, 0:8192], COL[:, 0:160],
                             [("sa", sa_ln1fm), ("sv", em.now("sv"))])

    wq_used = [0, 0]

    def meanabs_piece(wT, r, cl, acc_c, first):
        acc = COL[:, acc_c:acc_c + 1]
        tmp = COL[:, acc_c + 1:acc_c + 2]
        par = meanabs_piece.n % 2
        meanabs_piece.n += 1
        buf = WQ[:, par * 512:(par + 1) * 512]
        E("sync", lambda s: s.dma_start(
            out=buf, in_=wT[r * 128:(r + 1) * 128, cl * 512:(cl + 1) * 512]),
          waits=[("sv", wq_used[par])], inc=("lm", 16))
        lmv = em.now("lm")
        E("vector", lambda v: v.reduce_sum(
            tmp, buf, AX, apply_absolute_value=True),
          waits=[("lm", lmv)], inc=("sv", 1))
        if first:
            E("vector", lambda v: v.tensor_copy(acc, tmp), inc=("sv", 1))
        else:
            E("vector", lambda v: v.tensor_tensor(acc, acc, tmp, OP.add),
              inc=("sv", 1))
        wq_used[par] = em.now("sv")
    meanabs_piece.n = 0

    def meanabs_finish(nrows, ncols, acc_c, c_c):
        acc = COL[:, acc_c:acc_c + 1]
        sv_acc = em.now("sv")
        E("tensor", lambda t, acc=acc: t.matmul(
            PF[0:1, 0:1], acc, CC[:, 3:4], start=True, stop=True),
          waits=[("sv", sv_acc), ("sa", em.now("sa")), ("st", em.now("st"))],
          inc=("st", 1))
        E("scalar", lambda a, nrows=nrows, ncols=ncols: a.activation(
            COL[0:1, C_W + 4:C_W + 5], PF[0:1, 0:1], AF.Copy,
            scale=1.0 / (nrows * ncols)),
          waits=[("st", em.now("st"))], inc=("sa", 1))
        E("vector", lambda v: v.tensor_single_scalar(
            COL[0:1, C_W + 4:C_W + 5], COL[0:1, C_W + 4:C_W + 5], 1e-5, OP.max),
          waits=[("sa", em.now("sa")), ("sv", em.now("sv"))], inc=("sv", 1))
        E("tensor", lambda t: t.matmul(
            PF[:, 1:2], ONER[:], COL[0:1, C_W + 4:C_W + 5], start=True, stop=True),
          waits=[("sv", em.now("sv"))], inc=("st", 1))
        E("scalar", lambda a, c_c=c_c: a.copy(COL[:, c_c:c_c + 1], PF[:, 1:2]),
          waits=[("st", em.now("st")), ("sa", em.now("sa"))], inc=("sa", 1))
        sa_c = em.now("sa")
        E("vector", lambda v, c_c=c_c: v.tensor_single_scalar(
            COL[:, c_c + 1:c_c + 2], COL[:, c_c:c_c + 1], 0.5, OP.mult),
          waits=[("sa", sa_c), ("sv", em.now("sv"))], inc=("sv", 1))
        E("vector", lambda v, c_c=c_c: v.tensor_single_scalar(
            COL[:, c_c + 2:c_c + 3], COL[:, c_c:c_c + 1], -0.5, OP.mult),
          inc=("sv", 1))
        return em.now("sv")


    sa_po_evac = [0, 0]
    sa_ps = [0] * 4
    st_ps = [0] * 4
    et_use = [0] * 4
    st_pair_w = 0
    st_wbf_free = 0
    st_pv_prev = 0
    wbf = PAIR[:, PW:PW + 3072]

    E("vector", lambda v: v.tensor_copy(wbf, wst),
      waits=[("lq", 16), ("sv", sv_init)], inc=("sv", 1))
    sv_wbf_next = em.now("sv")
    E("sync", lambda s: s.dma_start(out=wst, in_=wqkv[1]),
      waits=[("sv", sv_wbf_next)], inc=("lq", 16))
    for c in range(NPAIR):
        sv_wbf = sv_wbf_next
        for which in range(2):
            coloff = 128 * which
            dstoff = PQ if which == 0 else PK
            for tb4 in range(4):
                bi = 2 * which + (tb4 % 2)
                bank = PS[bi]
                for dc in range(DC):
                    E("tensor", lambda t, dc=dc, coloff=coloff, tb4=tb4, bank=bank:
                      t.matmul(
                        bank[:], wbf[:, dc * 384 + coloff:dc * 384 + coloff + 128],
                        B1[:, dc * 2048 + tb4 * 512:dc * 2048 + (tb4 + 1) * 512],
                        start=(dc == 0), stop=(dc == DC - 1)),
                      waits=([("sv", sv_wbf), ("sv", sv_ln1fm),
                              ("sa", sa_ps[bi])] if dc == 0 else []),
                      inc=(("st", 1) if dc == DC - 1 else None))
                st_ps[bi] = em.now("st")
                E("scalar", lambda a, which=which, tb4=tb4, c=c, dstoff=dstoff,
                  bank=bank: a.activation(
                    PAIR[:, dstoff + tb4 * 512:dstoff + (tb4 + 1) * 512],
                    bank[:], AF.Identity,
                    bias=QKB[:, 2 * c + which:2 * c + which + 1],
                    scale=(0.125 if which == 0 else 1.0)),
                  waits=[("st", st_ps[bi])], inc=("sa", 1))
                sa_ps[bi] = em.now("sa")
        for tt in range(NT):
            bi = tt % 2
            bank = PS[bi]
            E("tensor", lambda t, c=c, bank=bank: t.matmul(
                bank[:, 0:128], ONER[:], BROW[:, c * 128:(c + 1) * 128],
                start=True, stop=False),
              waits=[("sa", sa_ps[bi]), ("st", st_pv_prev)])
            for dc in range(DC):
                E("tensor", lambda t, dc=dc, tt=tt, bank=bank: t.matmul(
                    bank[:, 0:128],
                    B1[:, dc * 2048 + tt * 128:dc * 2048 + (tt + 1) * 128],
                    wbf[:, dc * 384 + 256:dc * 384 + 384],
                    start=False, stop=(dc == DC - 1)),
                  inc=(("st", 1) if dc == DC - 1 else None))
            st_ps[bi] = em.now("st")
            E("scalar", lambda a, tt=tt, bank=bank: a.copy(
                PAIR[:, PV + tt * 130:PV + tt * 130 + 64], bank[:, 0:64]),
              waits=[("st", st_ps[bi]), ("sv", sv_init)], inc=("sa", 1))
            E("scalar", lambda a, tt=tt, bank=bank: a.copy(
                PAIR[:, PV + tt * 130 + 65:PV + tt * 130 + 129], bank[:, 64:128]),
              inc=("sa", 1))
            sa_ps[bi] = em.now("sa")
        sa_v = em.now("sa")
        st_wbf_free = em.now("st")
        if c + 1 < NPAIR:
            E("vector", lambda v: v.tensor_copy(wbf, wst),
              waits=[("lq", 16 * (c + 2)), ("st", st_wbf_free)],
              inc=("sv", 1))
            sv_wbf_next = em.now("sv")
            if c + 2 < NPAIR:
                E("sync", lambda s, c=c: s.dma_start(out=wst, in_=wqkv[c + 2]),
                  waits=[("sv", sv_wbf_next)], inc=("lq", 16))

        if STAGE == "qkv0" and c == 0:
            return dump_and_stop(PAIR[:, 0:8192], None,
                                 [("sa", sa_v), ("sv", em.now("sv"))])

        for qb in range(4):
            po_started = [False, False]
            for qi in range(qb * 4, qb * 4 + 4):
                for ki in ([qi - 1] if qi else []) + [qi]:
                    delta = qi - ki
                    for h2 in range(2):
                        bi = h2
                        bank = PS[bi]
                        E("tensor", lambda t, h2=h2, ki=ki, qi=qi, bank=bank:
                          t.matmul(
                            bank[:, 0:128],
                            PAIR[64 * h2:64 * h2 + 64, PK + ki * 128:PK + ki * 128 + 128],
                            PAIR[64 * h2:64 * h2 + 64, PQ + qi * 128:PQ + qi * 128 + 128],
                            start=True, stop=True),
                          waits=[("sa", sa_v), ("sa", sa_ps[bi])],
                          inc=("st", 1))
                        st_ps[bi] = em.now("st")
                        slot = 2 * h2 + (ki % 2)
                        et = ETB[:, slot * 128:(slot + 1) * 128]
                        E("scalar", lambda a, et=et, bank=bank, delta=delta:
                          a.activation(et, bank[:, 0:128], AF.Exp,
                                       bias=CC[:, delta:delta + 1]),
                          waits=[("st", st_ps[bi]), ("st", et_use[slot])],
                          inc=("sa", 1))
                        sa_ps[bi] = em.now("sa")
                        sa_exp = em.now("sa")
                        if delta == 0:
                            E("vector", lambda v, et=et: v.tensor_tensor(
                                et, et, DIAG[:], OP.mult),
                              waits=[("sa", sa_exp), ("sv", em.now("sv"))],
                              inc=("sv", 1))
                            dep = [("sv", em.now("sv"))]
                        else:
                            dep = [("sa", sa_exp)]
                        po = PO[h2]
                        first = not po_started[h2]
                        po_started[h2] = True
                        last = (qi == qb * 4 + 3) and (ki == qi)
                        E("tensor", lambda t, po=po, qi=qi, ki=ki, h2=h2, et=et,
                          first=first, last=last: t.matmul(
                            po[:, (qi % 4) * 65:(qi % 4) * 65 + 65], et,
                            PAIR[:, PV + ki * 130 + 65 * h2:PV + ki * 130 + 65 * h2 + 65],
                            start=first, stop=last),
                          waits=dep + [("sa", sa_po_evac[h2])],
                          inc=(("st", 1) if last else None))
                        et_use[slot] = em.now("st")
            st_po = em.now("st")
            st_pair_w = st_po
            st_pv_prev = st_po
            for h2 in range(2):
                po = PO[h2]
                E("vector", lambda v, po=po, h2=h2: v.reciprocal(
                    COL[:, C_RD + 4 * h2:C_RD + 4 * h2 + 4], po.ap()[:, 64:260:65]),
                  waits=[("st", st_po), ("sv", em.now("sv"))], inc=("sv", 1))
                sv_rd = em.now("sv")
                for s4 in range(4):
                    qi = qb * 4 + s4
                    hg = 2 * c + h2
                    E("scalar", lambda a, po=po, s4=s4, qi=qi, hg=hg, h2=h2:
                      a.activation(
                        B2[:, qi * 1024 + 64 * hg:qi * 1024 + 64 * hg + 64],
                        po[:, s4 * 65:s4 * 65 + 64], AF.Copy,
                        scale=COL[:, C_RD + 4 * h2 + s4:C_RD + 4 * h2 + s4 + 1]),
                      waits=[("sv", sv_rd), ("sa", em.now("sa"))], inc=("sa", 1))
                sa_po_evac[h2] = em.now("sa")
        for i in range(16):
            k = c * 16 + i
            if k < 64:
                meanabs_piece(w1T, k // 8, k % 8, C_W + 6, k == 0)
            else:
                k -= 64
                meanabs_piece(w2T, k // 2, k % 2, C_W + 8, k == 0)
    sa_attn = em.now("sa")

    if STAGE == "att0":
        return dump_and_stop(B2[:, 0:8192], COL[:, 0:160],
                             [("sa", sa_attn), ("sv", em.now("sv"))])

    h_done_st = [0, 0, 0, 0]
    sa_n1 = [0] * (NT + 1)
    sv_n1 = [0] * (NT + 1)

    def n1_tile(tt):
        w_main = tt // 2
        slot = w_main % 4
        xt = X[:, tt * D:(tt + 1) * D]
        scr = SCR[:, S_LN:S_LN + D]
        sa_st2 = ln_stats(tt, [("sv", sv_x2[tt])])
        ln_cols(tt, sa_st2)
        E("vector", lambda v, scr=scr, xt=xt, tt=tt: v.tensor_scalar(
            scr, xt, COL[:, C_M + tt:C_M + tt + 1], COL[:, C_R + tt:C_R + tt + 1],
            OP.subtract, OP.mult),
          waits=[("sv", em.now("sv"))], inc=("sv", 1))
        rmc = COL[:, C_RM + tt:C_RM + tt + 1]
        s1c = COL[:, C_S1 + tt:C_S1 + tt + 1]
        E("vector", lambda v, rmc=rmc, scr=scr: v.reduce_max(
            rmc, scr, AX, apply_absolute_value=True), inc=("sv", 1))
        E("vector", lambda v, rmc=rmc: v.tensor_single_scalar(
            rmc, rmc, 1e-5, OP.max), inc=("sv", 1))
        E("vector", lambda v, s1c=s1c, rmc=rmc: v.reciprocal(s1c, rmc),
          inc=("sv", 1))
        E("vector", lambda v, s1c=s1c: v.tensor_single_scalar(
            s1c, s1c, 127.0, OP.mult), inc=("sv", 1))
        for hh in range(4):
            E("vector", lambda v, scr=scr, s1c=s1c, hh=hh: v.tensor_scalar(
                WI[:, 0:256], scr[:, hh * 256:(hh + 1) * 256], s1c,
                COL[:, C_U23:C_U23 + 1], OP.mult, OP.add),
              waits=[("sa", em.now("sa")), ("sv", em.now("sv"))], inc=("sv", 1))
            sv_wi = em.now("sv")
            E("scalar", lambda a, hh=hh: a.activation(
                N1T[:, hh * 256:(hh + 1) * 256], WI[:, 0:256], AF.Copy,
                bias=-MAGIC),
              waits=[("sv", sv_wi), ("st", em.now("st")), ("sa", em.now("sa"))],
              inc=("sa", 1))
        win = N1B[:, slot * 2064:slot * 2064 + 2064]
        sa_fin = em.now("sa")
        for fc in range(DC):
            p = PT[:, (fc % 2) * 128:(fc % 2) * 128 + 128]
            E("tensor", lambda t, p=p, fc=fc: t.transpose(
                p, N1T[:, fc * 128:(fc + 1) * 128], IDBF[:]),
              waits=[("sa", sa_fin), ("sv", pt_sv[0]),
                     ("st", h_done_st[slot])],
              inc=("st", 1))
            stv = em.now("st")
            dst0 = fc * 258 + 1 + (tt % 2) * 128
            E("vector", lambda v, p=p, win=win, dst0=dst0: v.tensor_copy(
                win[:, dst0:dst0 + 128], p),
              waits=[("st", stv)], inc=("sv", 1))
            if tt % 2 == 0 and w_main >= 1:
                prv = N1B[:, ((w_main - 1) % 4) * 2064:((w_main - 1) % 4) * 2064 + 2064]
                E("vector", lambda v, p=p, prv=prv, fc=fc: v.tensor_copy(
                    prv[:, fc * 258 + 257:fc * 258 + 258], p[:, 0:1]),
                  inc=("sv", 1))
            pt_sv[0] = em.now("sv")
        if tt == 0:
            for fc in range(DC):
                E("scalar", lambda a, win=win, fc=fc: a.copy(
                    win[:, fc * 258:fc * 258 + 1], CC[:, 4:5]), inc=("sa", 1))
        if tt == NT - 1:
            for fc in range(DC):
                E("scalar", lambda a, win=win, fc=fc: a.copy(
                    win[:, fc * 258 + 257:fc * 258 + 258], CC[:, 4:5]),
                  inc=("sa", 1))
        E("tensor", lambda t, tt=tt: t.transpose(
            PF[0:128, 0:1].cast(F32) if False else PF[0:1, 0:128],
            COL[:, C_S1 + tt:C_S1 + tt + 1], ID32[:]),
          waits=[("sa", em.now("sa")), ("sv", em.now("sv"))], inc=("st", 1)) \
            if False else None
        sa_n1[tt] = em.now("sa")
        sv_n1[tt] = em.now("sv")


    meanabs_finish(D, CH, C_W + 6, C_C1)
    sv_c1 = em.now("sv")
    meanabs_finish(CH, D, C_W + 8, C_C2)
    sv_c2 = em.now("sv")

    pos = SCR[:, S_QP:S_QP + 512]

    def wquant_piece(wT, r, cl, c_c, sv_c, store):
        par = wquant_piece.n % 2
        wquant_piece.n += 1
        buf = WQ[:, par * 512:(par + 1) * 512]
        E("sync", lambda s: s.dma_start(
            out=buf, in_=wT[r * 128:(r + 1) * 128, cl * 512:(cl + 1) * 512]),
          waits=[("sv", wq_used[par])], inc=("lm", 16))
        lmv = em.now("lm")
        E("vector", lambda v: v.tensor_single_scalar(
            pos, buf, COL[:, c_c + 1:c_c + 2], OP.is_gt),
          waits=[("lm", lmv), ("sv", sv_c)], inc=("sv", 1))
        E("vector", lambda v: v.scalar_tensor_tensor(
            pos, buf, COL[:, c_c + 2:c_c + 3], pos, OP.is_lt, OP.subtract),
          inc=("sv", 1))
        wq_used[par] = em.now("sv")
        store(r, cl)
    wquant_piece.n = 0

    def store_m1(r, cl):
        col0 = cl * 512
        if col0 + 512 <= 2048:
            E("vector", lambda v: v.tensor_copy(
                B1[:, r * 2048 + col0:r * 2048 + col0 + 512], pos),
              waits=[("st", em.now("st"))], inc=("sv", 1))
        else:
            E("vector", lambda v: v.tensor_copy(ETB[:, 0:512], pos),
              waits=[("lf", em.now("lf")), ("st", em.now("st"))],
              inc=("sv", 1))
            svv = em.now("sv")
            for i in range(4):
                cb = (col0 - 2048) // 128 + i
                E("scalar", lambda a, cb=cb, i=i, r=r: a.dma_start(
                    out=m1h[cb, :, r, :],
                    in_=ETB[:, i * 128:(i + 1) * 128]),
                  waits=[("sv", svv)], inc=("lf", 16))

    def store_m2(r, cl):
        cb, j0 = r, cl * 512
        if cb < 16:
            E("vector", lambda v: v.tensor_copy(
                B2[:, cb * 1024 + j0:cb * 1024 + j0 + 512], pos),
              waits=[("st", em.now("st"))], inc=("sv", 1))
        else:
            E("vector", lambda v, j0=j0: v.tensor_copy(
                PAIR[:, PM2S + j0:PM2S + j0 + 512], pos),
              waits=[("lf", em.now("lf"))], inc=("sv", 1))
            svv = em.now("sv")
            if j0 == 512:
                E("scalar", lambda a, cb=cb: a.dma_start(
                    out=m2h[cb - 16, :, :], in_=PAIR[:, PM2S:PM2S + 1024]),
                  waits=[("sv", svv)], inc=("lf", 16))

    def s1_rows(tt0):
        E("tensor", lambda t: t.transpose(PF[0:16, 0:128], COL[:, C_S1:C_S1 + 16],
                                          ID32[:]),
          waits=[("sa", em.now("sa")), ("sv", em.now("sv")),
                 ("lr", em.now("lr"))],
          inc=("st", 1))
        stv = em.now("st")
        E("scalar", lambda a: a.copy(WQ[0:16, 0:128], PF[0:16, 0:128]),
          waits=[("st", stv)], inc=("sa", 1))
        sav = em.now("sa")
        for tt in (tt0, tt0 + 1):
            E("sync", lambda s, tt=tt: s.dma_start(
                out=S2R[0:1, 1 + tt * 128:1 + (tt + 1) * 128],
                in_=WQ[tt % 16:tt % 16 + 1, 0:128]),
              waits=[("sa", sav)], inc=("lr", 16))
        return ("lr", em.now("lr"))

    E("sync", lambda s: s.dma_start(out=BROW[:], in_=borow),
      waits=[("st", st_pair_w)], inc=("lw", 16))
    lw_bo = em.now("lw")
    for oc in range(DC):
        buf = SCR[:, S_WOS + (oc % 2) * 1024:S_WOS + (oc % 2) * 1024 + 1024]
        E("sync", lambda s, oc=oc, buf=buf: s.dma_start(
            out=buf, in_=woutT[oc * 128:(oc + 1) * 128, :]),
          waits=[("sv", em.now("sv"))], inc=("lw", 16))
        E("vector", lambda v, oc=oc, buf=buf: v.tensor_copy(
            PAIR[:, PWO + oc * 1024:PWO + (oc + 1) * 1024], buf),
          waits=[("lw", em.now("lw")), ("st", st_pair_w), ("sv", em.now("sv"))],
          inc=("sv", 1))
    sv_wo = em.now("sv")

    sv_x2 = [0] * NT
    sv_ps_free = [em.now("sv")] * 4
    of_done = [0, 0]
    st_of_used = [0, 0]

    def emit_of_transposes(tt):
        par = tt % 2
        dv = 0
        for oc in range(DC):
            dv = transpose_to(
                B2[:, tt * 1024 + oc * 128:tt * 1024 + (oc + 1) * 128],
                PAIR[:, POF + par * 1024 + oc * 128:POF + par * 1024 + (oc + 1) * 128],
                oc % 2,
                waits=[("sa", sa_attn), ("st", st_of_used[par])])
        of_done[par] = dv

    emit_of_transposes(0)
    for tt in range(NT):
        par = tt % 2
        if tt + 1 < NT:
            emit_of_transposes(tt + 1)
        for jb in range(2):
            bi = jb
            bank = PS[bi]
            E("tensor", lambda t, jb=jb, bank=bank: t.matmul(
                bank[:], ONER[:], BROW[:, jb * 512:(jb + 1) * 512],
                start=True, stop=False),
              waits=[("sv", of_done[par]), ("sv", sv_wo), ("sv", sv_ps_free[bi]),
                     ("lw", lw_bo)])
            for oc in range(DC):
                E("tensor", lambda t, oc=oc, jb=jb, bank=bank, par=par: t.matmul(
                    bank[:],
                    PAIR[:, POF + par * 1024 + oc * 128:POF + par * 1024 + (oc + 1) * 128],
                    PAIR[:, PWO + oc * 1024 + jb * 512:PWO + oc * 1024 + jb * 512 + 512],
                    start=False, stop=(oc == DC - 1)),
                  inc=(("st", 1) if oc == DC - 1 else None))
            if jb == 1:
                st_of_used[par] = em.now("st")
            st_ps[bi] = em.now("st")
            E("vector", lambda v, tt=tt, jb=jb, bank=bank: v.tensor_tensor(
                X[:, tt * D + jb * 512:tt * D + (jb + 1) * 512],
                X[:, tt * D + jb * 512:tt * D + (jb + 1) * 512],
                bank[:], OP.add),
              waits=[("st", st_ps[bi])], inc=("sv", 1))
            sv_ps_free[bi] = em.now("sv")
        sv_x2[tt] = em.now("sv")
        E("sync", lambda s, tt=tt: s.dma_start(
            out=out[tt * 128:(tt + 1) * 128, :], in_=X[:, tt * D:(tt + 1) * D]),
          waits=[("sv", sv_x2[tt])], inc=("so", 16))
        if tt < 8:
            n1_tile(tt)
        for i in range(4):
            k = tt * 4 + i
            wquant_piece(w1T, k // 8, k % 8, C_C1, sv_c1, store_m1)
        for cl in range(2):
            wquant_piece(w2T, tt, cl, C_C2, sv_c2, store_m2)
            wquant_piece(w2T, 16 + tt, cl, C_C2, sv_c2, store_m2)
    so_x2 = em.now("so")
    sv_outproj = em.now("sv")

    if STAGE == "x2":
        E("gpsimd", None, waits=[("so", em.now("so"))])
        with nc.Block() as block:
            em.replay(block)
        ctx.close()
        return nc


    sv_m1 = sv_m2 = em.now("sv")
    lf_wq = em.now("lf")

    sa_s1r = [("lr", 0)] * NTB
    for w in range(4):
        sa_s1r[w] = s1_rows(2 * w)

    sv_hs_free, sg_hs_free = em.now("sv"), em.now("sg")
    sv_w2_prev = em.now("sv")
    st_w2 = [em.now("st")] * 4
    st_m1buf = [em.now("st")] * 2
    st_m2buf = [em.now("st")] * 2
    st_ysc = 0
    sv_seed_ok = 0
    HS = 0
    sb1 = SCR[:, S_BC:S_BC + 258]
    sb2 = SCR[:, S_BC + 300:S_BC + 300 + TBW]
    lr_rb = 0

    col0_sa = [0] * (NTB + 1)
    for tb in range(NTB):
        slot = tb % 4
        win = N1B[:, slot * 2064:slot * 2064 + 2064]
        if tb + 1 < NTB:
            nslot = (tb + 1) % 4
            nwin = N1B[:, nslot * 2064:nslot * 2064 + 2064]
            E("scalar", lambda a, win=win, nwin=nwin: a.copy(
                nwin.rearrange("p (f c) -> p f c", f=8, c=258)[:, :, 0:1],
                win.rearrange("p (f c) -> p f c", f=8, c=258)[:, :, 256:257]),
              waits=[("sa", sa_n1[2 * tb + 1]), ("sv", sv_n1[2 * tb + 1]),
                     ("sa", sa_n1[min(2 * tb + 3, NT - 1)]),
                     ("sv", sv_n1[min(2 * tb + 3, NT - 1)]),
                     ("st", h_done_st[nslot]), ("sa", em.now("sa"))],
              inc=("sa", 1))
            col0_sa[tb + 1] = em.now("sa")
        win_ready = [("sa", sa_n1[min(2 * tb + 2, NT - 1)]),
                     ("sv", sv_n1[min(2 * tb + 2, NT - 1)]),
                     ("sa", sa_n1[2 * tb + 1]), ("sv", sv_n1[2 * tb + 1]),
                     ("sa", col0_sa[tb]), sa_s1r[tb]]
        E("tensor", lambda t, tb=tb: t.matmul(
            PF[:, 0:258], ONER[:], S2R[0:1, tb * TBW:tb * TBW + 258],
            start=True, stop=True),
          waits=win_ready + [("sa", em.now("sa")), ("sv", em.now("sv")),
                             ("st", em.now("st"))],
          inc=("st", 1))
        stv = em.now("st")
        E("vector", lambda v: v.reciprocal(sb1, PF[:, 0:258]),
          waits=[("st", stv), ("sv", em.now("sv"))], inc=("sv", 1))
        E("vector", lambda v: v.tensor_scalar(
            sb1, sb1, COL[:, C_C1:C_C1 + 1], -1.0, OP.mult, OP.mult),
          waits=[("sv", sv_m1)], inc=("sv", 1))
        sv_sb1 = em.now("sv")

        sv_run = 0
        sv_hsc = [0, 0]
        cb_order = list(range(NCB))
        for cbi, cb in enumerate(cb_order):
            par = cbi % 2
            hbank = PO[par]
            if cb >= 16:
                mbuf = PAIR[:, PM1S + par * 1024:PM1S + par * 1024 + 1024]
                E("sync", lambda s, cb=cb, mbuf=mbuf: s.dma_start(
                    out=mbuf, in_=m1h[cb - 16].rearrange("p f c -> p (f c)")),
                  waits=[("lf", lf_wq), ("sv", sv_m1), ("st", st_m1buf[par])],
                  inc=("ls", 16))
                ls_m1 = em.now("ls")
            for fc in range(DC):
                if cb < 16:
                    lhsT = B1[:, fc * 2048 + cb * 128:fc * 2048 + (cb + 1) * 128]
                    lw = [("sv", sv_m1)] if fc == 0 else []
                else:
                    lhsT = PAIR[:, PM1S + par * 1024 + fc * 128:
                                PM1S + par * 1024 + (fc + 1) * 128]
                    lw = [("ls", ls_m1)] if fc == 0 else []
                E("tensor", lambda t, lhsT=lhsT, fc=fc, hbank=hbank, win=win:
                  t.matmul(hbank[:, 0:258], lhsT, win[:, fc * 258:(fc + 1) * 258],
                           start=(fc == 0), stop=(fc == DC - 1)),
                  waits=(lw + win_ready +
                         [("sv", sv_hsc[par]), ("sa", sa_po_evac[par])])
                  if fc == 0 else lw,
                  inc=(("st", 1) if fc == DC - 1 else None))
            st_h = em.now("st")
            if cb >= 16:
                st_m1buf[par] = st_h
            h_done_st[slot] = st_h
            hsc = SCR[:, S_HSC + par * 260:S_HSC + par * 260 + 258]
            E("vector", lambda v, hsc=hsc, hbank=hbank: v.tensor_tensor(
                hsc, hbank[:, 0:258], sb1, OP.mult),
              waits=[("st", st_h), ("sv", sv_sb1), ("sg", em.now("sg")),
                     ("sv", em.now("sv"))],
              inc=("sv", 1))
            sv_hsc[par] = em.now("sv")
            ub = SCR[:, S_UB + par * 260:S_UB + par * 260 + 256]
            E("vector", lambda v, ub=ub, hsc=hsc, cb=cb: v.tensor_single_scalar(
                ub, hsc[:, 0:256], CW[:, 3 * cb:3 * cb + 1], OP.mult),
              inc=("sv", 1))
            E("vector", lambda v, ub=ub, hsc=hsc, cb=cb: v.scalar_tensor_tensor(
                ub, hsc[:, 1:257], CW[:, 3 * cb + 1:3 * cb + 2], ub,
                OP.mult, OP.add),
              inc=("sv", 1))
            E("vector", lambda v, ub=ub, hsc=hsc, cb=cb: v.scalar_tensor_tensor(
                ub, hsc[:, 2:258], CW[:, 3 * cb + 2:3 * cb + 3], ub,
                OP.mult, OP.add),
              inc=("sv", 1))
            sv_conv = em.now("sv")
            hss = X[:, HS + cb * TBW:HS + (cb + 1) * TBW]
            E("scalar", lambda a, hss=hss, ub=ub, cb=cb: a.activation(
                hss, ub, AF.Silu, bias=CBc[:, cb:cb + 1]),
              waits=[("sv", sv_conv), ("sv", sv_hs_free), ("sa", em.now("sa"))],
              inc=("sa", 1))
            sa_silu = em.now("sa")
        hsview = X[:, 0:NCB * TBW].rearrange("p (cb t) -> p t cb", cb=NCB, t=TBW)
        E("vector", lambda v, hsview=hsview: v.reduce_max(
            RMX[:, 0:TBW], hsview, AX, apply_absolute_value=True),
          waits=[("sa", sa_silu), ("st", st_ysc), ("sv", em.now("sv"))],
          inc=("sv", 1))
        sv_run = em.now("sv")
        for half in range(2):
            E("tensor", lambda t, half=half: t.transpose(
                PF[:, half * 128:(half + 1) * 128],
                RMX[:, half * 128:(half + 1) * 128], ID32[:]),
              waits=[("sv", sv_run), ("st", em.now("st"))],
              inc=("st", 1))
        st_tr = em.now("st")
        rmc = COL[:, C_X:C_X + 2]
        for half in range(2):
            E("vector", lambda v, half=half: v.reduce_max(
                rmc[:, half:half + 1], PF[:, half * 128:(half + 1) * 128], AX),
              waits=[("st", st_tr)], inc=("sv", 1))
        E("vector", lambda v, rmc=rmc: v.tensor_single_scalar(
            rmc, rmc, 1e-5, OP.max), inc=("sv", 1))
        sv_rm = em.now("sv")
        ysc = COL[:, C_Y + (tb % 2) * 2:C_Y + (tb % 2) * 2 + 2]
        E("vector", lambda v, ysc=ysc, rmc=rmc: v.tensor_scalar(
            ysc, rmc, COL[:, C_C2:C_C2 + 1], -1.0 / 127.0, OP.mult, OP.mult),
          waits=[("sv", sv_m2)], inc=("sv", 1))
        sv_ysc = em.now("sv")
        s2c = COL[:, C_X + 2:C_X + 4]
        E("vector", lambda v, s2c=s2c, rmc=rmc: v.reciprocal(s2c, rmc),
          inc=("sv", 1))
        E("vector", lambda v, s2c=s2c: v.tensor_single_scalar(
            s2c, s2c, 127.0, OP.mult), inc=("sv", 1))
        sv_s2 = em.now("sv")
        for half in range(2):
            E("tensor", lambda t, half=half, s2c=s2c: t.matmul(
                PF[0:1, 256 + half * 128:256 + (half + 1) * 128],
                s2c[:, half:half + 1], ID32[:], start=True, stop=True),
              waits=[("sv", sv_s2)], inc=("st", 1))
        stv = em.now("st")
        row = WQ[0:1, 256:256 + TBW]
        E("scalar", lambda a, row=row: a.copy(row, PF[0:1, 256:512]),
          waits=[("st", stv)], inc=("sa", 1))
        sa_row2 = em.now("sa")
        E("tensor", lambda t, row=row: t.matmul(
            PF[:, 0:TBW], ONER[:], row, start=True, stop=True),
          waits=[("sa", sa_row2), ("sv", em.now("sv"))],
          inc=("st", 1))
        stv = em.now("st")
        st_ysc = stv
        E("scalar", lambda a: a.copy(sb2, PF[:, 0:TBW]),
          waits=[("st", stv)], inc=("sa", 1))
        sa_s2b = em.now("sa")
        sa_n2cb = [0] * NCB
        for cb in cb_order:
            hss = X[:, HS + cb * TBW:HS + (cb + 1) * TBW]
            E("vector", lambda v, hss=hss: v.tensor_tensor(hss, hss, sb2, OP.mult),
              waits=[("sa", sa_s2b), ("sv", em.now("sv"))], inc=("sv", 1))
            E("vector", lambda v, hss=hss: v.tensor_scalar(
                WI[:, 256:256 + TBW], hss, 1.0, COL[:, C_U23:C_U23 + 1],
                OP.mult, OP.add),
              waits=[("sa", em.now("sa"))], inc=("sv", 1))
            sv_wiq = em.now("sv")
            E("scalar", lambda a, cb=cb: a.activation(
                PAIR[:, PN2 + cb * TBW:PN2 + (cb + 1) * TBW], WI[:, 256:256 + TBW],
                AF.Copy, bias=-MAGIC),
              waits=[("sv", sv_wiq), ("sa", em.now("sa"))], inc=("sa", 1))
            sa_n2cb[cb] = em.now("sa")
        sa_n2 = em.now("sa")
        sv_hs_free = em.now("sv")

        if STAGE == "ffn0" and tb == DBG_TB:
            E("sync", lambda s: s.dma_start(out=dbgh[:, 0:8192], in_=X[:, 0:8192]),
              waits=[("sv", em.now("sv")), ("sa", em.now("sa"))], inc=("so", 16))
            E("sync", lambda s: s.dma_start(out=dbgf[0:1, 0:512], in_=S2R[0:1, 0:512]),
              waits=[("sv", em.now("sv"))], inc=("so", 16))
            E("sync", lambda s: s.dma_start(out=dbgf[1:2, 0:160], in_=COL[0:1, 0:160]),
              inc=("so", 16))
            E("sync", lambda s: s.dma_start(out=dbgf[2:128, 0:160],
                                            in_=COL[1:127, 0:160]),
              inc=("so", 16))
            E("sync", lambda s: s.dma_start(out=dbgf[0:128, 160:418],
                                            in_=SCR[:, 2100:2358]),
              inc=("so", 16))
            return dump_and_stop(N1B[:, (tb % 4) * 2064:(tb % 4) * 2064 + 2064], None,
                                 [("sa", sa_n2), ("sv", em.now("sv"))])

        for cbi, cb in enumerate(cb_order):
            if cb < 16:
                rhs = B2[:, cb * 1024:(cb + 1) * 1024]
                rw = [("sv", sv_m2)] if cb == 0 else []
            else:
                mbuf = PAIR[:, PM2S:PM2S + 1024]
                E("sync", lambda s, cb=cb, mbuf=mbuf: s.dma_start(
                    out=mbuf, in_=m2h[cb - 16]),
                  waits=[("lf", lf_wq), ("sv", sv_m2), ("st", st_m2buf[0])],
                  inc=("ls", 16))
                rhs = mbuf
                rw = [("ls", em.now("ls"))]
            for ht in range(2):
                for jb in range(2):
                    bank = PS[2 * ht + jb]
                    E("tensor", lambda t, cb=cb, ht=ht, jb=jb, bank=bank, rhs=rhs,
                      cbi=cbi:
                      t.matmul(
                        bank[:],
                        PAIR[:, PN2 + cb * TBW + ht * 128:PN2 + cb * TBW + ht * 128 + 128],
                        rhs[:, jb * 512:(jb + 1) * 512],
                        start=(cbi == 0), stop=(cbi == NCB - 1)),
                      waits=(rw + [("sa", sa_n2cb[cb]), ("sv", sv_w2_prev)])
                      if (ht == 0 and jb == 0) else [],
                      inc=(("st", 1) if (ht == 1 and jb == 1) else None))
            if cb >= 16:
                st_m2buf[0] = em.now("st")
        st_w2e = em.now("st")
        if STAGE == "w2d" and tb == DBG_TB:
            for bi in range(4):
                E("vector", lambda v, bi=bi: v.tensor_copy(
                    SCR[:, bi * 512:(bi + 1) * 512], PS[bi][:]),
                  waits=[("st", st_w2e), ("sv", em.now("sv"))], inc=("sv", 1))
            svv = em.now("sv")
            E("sync", lambda s: s.dma_start(out=dbgh[:, 0:2048],
                                            in_=SCR[:, 0:2048]),
              waits=[("sv", svv)], inc=("so", 16))
            E("sync", lambda s: s.dma_start(out=dbg[:, 0:8192],
                                            in_=PAIR[:, PN2:PN2 + 8192]),
              waits=[("sa", sa_n2)], inc=("so", 16))
            E("sync", lambda s: s.dma_start(out=dbgf[0:1, 0:160],
                                            in_=COL[0:1, 0:160]),
              inc=("so", 16))
            E("gpsimd", None, waits=[("so", em.now("so"))])
            with nc.Block() as block:
                em.replay(block)
            ctx.close()
            return nc
        for ht in range(2):
            tt = 2 * tb + ht
            E("sync", lambda s, tt=tt: s.dma_start(
                out=X2RB[:, 0:D], in_=out[tt * 128:(tt + 1) * 128, :]),
              waits=[("so", em.now("so")), ("sv", em.now("sv"))],
              inc=("lr", 16))
            lr_rb = em.now("lr")
            for jb in range(2):
                bank = PS[2 * ht + jb]
                ysc1 = COL[:, C_Y + (tb % 2) * 2 + ht:C_Y + (tb % 2) * 2 + ht + 1]
                E("vector", lambda v, bank=bank, jb=jb, ysc1=ysc1:
                  v.scalar_tensor_tensor(
                    X2RB[:, jb * 512:(jb + 1) * 512],
                    bank[:], ysc1,
                    X2RB[:, jb * 512:(jb + 1) * 512],
                    OP.mult, OP.add),
                  waits=[("st", st_w2e), ("lr", lr_rb), ("sv", sv_ysc),
                         ("sv", em.now("sv"))],
                  inc=("sv", 1))
            sv_y = em.now("sv")
            E("sync", lambda s, tt=tt: s.dma_start(
                out=out[tt * 128:(tt + 1) * 128, :], in_=X2RB[:, 0:D]),
              waits=[("sv", sv_y)], inc=("so", 16))
        sv_w2_prev = em.now("sv")

        if 2 * tb + 8 < NT:
            n1_tile(2 * tb + 8)
            n1_tile(2 * tb + 9)
            sa_s1r[tb + 4] = s1_rows(2 * tb + 8)

    E("gpsimd", None, waits=[("so", em.now("so"))])

    with nc.Block() as block:
        em.replay(block)
    ctx.close()
    return nc


_BF = ml_dtypes.bfloat16


def kernel(L0_attn_x, L0_attn_w_qkv, L0_attn_b_qkv, L0_attn_w_out, L0_attn_b_out,
           L0_ffn_w1, L0_ffn_conv_w, L0_ffn_conv_b, L0_ffn_w2):
    x = np.ascontiguousarray(np.asarray(L0_attn_x, dtype=np.float32))
    w_qkv = np.asarray(L0_attn_w_qkv, dtype=np.float32)
    b_qkv = np.asarray(L0_attn_b_qkv, dtype=np.float32)
    w_out = np.asarray(L0_attn_w_out, dtype=np.float32)
    b_out = np.asarray(L0_attn_b_out, dtype=np.float32)
    w1 = np.asarray(L0_ffn_w1, dtype=np.float32)
    conv_w = np.asarray(L0_ffn_conv_w, dtype=np.float32)
    conv_b = np.asarray(L0_ffn_conv_b, dtype=np.float32)
    w2 = np.asarray(L0_ffn_w2, dtype=np.float32)

    wqkvT = w_qkv.T
    wqkv_pair = np.ascontiguousarray(np.stack([
        np.concatenate([wqkvT[:, 128 * c:128 * c + 128],
                        wqkvT[:, 1024 + 128 * c:1024 + 128 * c + 128],
                        wqkvT[:, 2048 + 128 * c:2048 + 128 * c + 128]],
                       axis=1).reshape(8, 128, 384).transpose(1, 0, 2).reshape(128, 3072)
        for c in range(NPAIR)]))
    qkcols = np.zeros((128, 16), dtype=np.float32)
    for c in range(NPAIR):
        qkcols[:, 2 * c] = b_qkv[128 * c:128 * c + 128]
        qkcols[:, 2 * c + 1] = b_qkv[1024 + 128 * c:1024 + 128 * c + 128]
    bvrow = np.ascontiguousarray(b_qkv[2048:3072])[None, :]
    borow = np.ascontiguousarray(b_out)[None, :]
    woutT_a = np.ascontiguousarray(w_out.T)
    w1T_a = np.ascontiguousarray(w1.T)
    w2T_a = np.ascontiguousarray(w2.T)
    cw = conv_w.reshape(CH, 3)
    cwcols = np.zeros((128, 96), dtype=np.float32)
    cbcols = np.zeros((128, 32), dtype=np.float32)
    for cb in range(NCB):
        for k in range(3):
            cwcols[:, 3 * cb + k] = cw[cb * 128:(cb + 1) * 128, k]
        cbcols[:, cb] = conv_b[cb * 128:(cb + 1) * 128]
    kp = np.arange(128, dtype=np.float32)[:, None]
    qf = np.arange(128, dtype=np.float32)[None, :]
    ccols = np.zeros((128, 8), dtype=np.float32)
    ccols[:, 0:1] = kp - 64.0
    ccols[:, 1:2] = kp - 192.0
    ccols[:, 2] = 8388608.0
    ccols[:, 3] = 1.0
    cdiag = (kp <= qf).astype(_BF)
    cid32 = np.eye(128, dtype=np.float32)
    cidbf = np.eye(128, dtype=np.float32).astype(_BF)
    conerow = np.ones((1, 128), dtype=np.float32)

    nc = build_program()
    in_maps = [{
        "xin": np.ascontiguousarray(x[b]),
        "wqkv": wqkv_pair, "woutT": woutT_a, "w1T": w1T_a, "w2T": w2T_a,
        "qkcols": qkcols, "bvrow": bvrow, "borow": borow,
        "cwcols": cwcols, "cbcols": cbcols,
        "cid32": cid32, "cidbf": cidbf, "cdiag": cdiag, "ccols": ccols,
        "conerow": conerow,
    } for b in range(8)]
    res = run_bass_kernel_spmd(nc, in_maps, core_ids=list(range(8)))
    return np.stack([r["out"] for r in res.results])


# version 11
# speedup vs baseline: 1.0579x; 1.0579x over previous
import numpy as np
import ml_dtypes
import concourse.bass as bass
import concourse.mybir as mybir
from concourse.bass_utils import run_bass_kernel_spmd

F32 = mybir.dt.float32
BF16 = mybir.dt.bfloat16
I32 = mybir.dt.int32
AX = mybir.AxisListType.X
OP = mybir.AluOpType
AF = mybir.ActivationFunctionType

S = 2048
D = 1024
NT = 16
DC = 8
NPAIR = 8
CH = 4096
NCB = 32
NTB = 8
TBW = 256
MAGIC = 12582912.0
STAGE = "full"
DBG_NOVBIAS = False
DBG_TB = 0


class Emitter:
    ENGINES = ("sync", "vector", "scalar", "tensor", "gpsimd")
    AUTO = {"vector": "sv", "scalar": "sa", "gpsimd": "sg"}

    def __init__(self):
        self.events = {e: [] for e in self.ENGINES}
        self.counts = {}
        self.handles = {}
        self.last_wait = {e: {} for e in self.ENGINES}

    def add_sem(self, name, handle):
        self.handles[name] = handle
        self.counts[name] = 0

    def emit(self, eng, fn=None, waits=(), inc=None):
        auto = self.AUTO.get(eng)
        if fn is not None and auto is not None:
            waits = list(waits) + [(auto, self.counts[auto])]
            if inc is None:
                inc = (auto, 1)
        evs = []
        for s, v in waits:
            if v > 0 and self.last_wait[eng].get(s, -1) < v:
                evs.append((s, v))
                self.last_wait[eng][s] = v
        self.events[eng].append((evs, fn, inc))
        if inc is not None:
            self.counts[inc[0]] += inc[1]

    def now(self, sem):
        return self.counts[sem]

    def replay(self, block):
        handles = self.handles
        for engname in self.ENGINES:
            evlist = self.events[engname]

            def body(eng, evlist=evlist):
                for waits, fn, inc in evlist:
                    for (s, v) in waits:
                        eng.wait_ge(handles[s], v)
                    if fn is not None:
                        inst = fn(eng)
                        if inc is not None:
                            inst.then_inc(handles[inc[0]], inc[1])
            getattr(block, engname)(body)


def build_program():
    nc = bass.Bass("TRN2", target_bir_lowering=False,
                   detect_race_conditions=False)

    xin = nc.dram_tensor("xin", [S, D], F32, kind="ExternalInput").ap()
    wqkv = nc.dram_tensor("wqkv", [NPAIR, 128, 3072], F32, kind="ExternalInput").ap()
    woutT = nc.dram_tensor("woutT", [D, D], F32, kind="ExternalInput").ap()
    w1T = nc.dram_tensor("w1T", [D, CH], F32, kind="ExternalInput").ap()
    w2T = nc.dram_tensor("w2T", [CH, D], F32, kind="ExternalInput").ap()
    qkcols = nc.dram_tensor("qkcols", [128, 16], F32, kind="ExternalInput").ap()
    bvrow = nc.dram_tensor("bvrow", [1, D], F32, kind="ExternalInput").ap()
    borow = nc.dram_tensor("borow", [1, D], F32, kind="ExternalInput").ap()
    cwcols = nc.dram_tensor("cwcols", [128, 96], F32, kind="ExternalInput").ap()
    cbcols = nc.dram_tensor("cbcols", [128, 32], F32, kind="ExternalInput").ap()
    cid32 = nc.dram_tensor("cid32", [128, 128], F32, kind="ExternalInput").ap()
    cidbf = nc.dram_tensor("cidbf", [128, 128], BF16, kind="ExternalInput").ap()
    cdiag = nc.dram_tensor("cdiag", [128, 128], BF16, kind="ExternalInput").ap()
    ccols = nc.dram_tensor("ccols", [128, 8], F32, kind="ExternalInput").ap()
    conerow = nc.dram_tensor("conerow", [1, 128], F32, kind="ExternalInput").ap()
    out = nc.dram_tensor("out", [S, D], F32, kind="ExternalOutput").ap()
    dbg = nc.dram_tensor("dbg", [128, 8192], BF16, kind="ExternalOutput").ap()
    dbgf = nc.dram_tensor("dbgf", [128, 512], F32, kind="ExternalOutput").ap()
    dbgh = nc.dram_tensor("dbgh", [128, 8192], F32, kind="ExternalOutput").ap()
    m1h = nc.dram_tensor("m1h", [16, 128, 8, 128], BF16, kind="Internal").ap()
    m2h = nc.dram_tensor("m2h", [16, 128, D], BF16, kind="Internal").ap()

    from contextlib import ExitStack
    ctx = ExitStack()
    e = ctx.enter_context

    X = e(nc.sbuf_tensor("X", [128, NT * D], F32))
    SCR = e(nc.sbuf_tensor("SCR", [128, 4096], F32))
    WQ = e(nc.sbuf_tensor("WQ", [128, 1024], F32))
    COL = e(nc.sbuf_tensor("COL", [128, 160], F32))
    QKB = e(nc.sbuf_tensor("QKB", [128, 16], F32))
    CW = e(nc.sbuf_tensor("CW", [128, 96], F32))
    CBc = e(nc.sbuf_tensor("CBc", [128, 32], F32))
    CC = e(nc.sbuf_tensor("CC", [128, 8], F32))
    ID32 = e(nc.sbuf_tensor("ID32", [128, 128], F32))
    ONER = e(nc.sbuf_tensor("ONER", [1, 128], F32))
    BROW = e(nc.sbuf_tensor("BROW", [1, D], F32))
    S2R = e(nc.sbuf_tensor("S2R", [1, S + 2], F32))
    RMX = e(nc.sbuf_tensor("RMX", [128, TBW], F32))
    B1 = e(nc.sbuf_tensor("B1", [128, 16384], BF16))
    B2 = e(nc.sbuf_tensor("B2", [128, 16384], BF16))
    PAIR = e(nc.sbuf_tensor("PAIR", [128, 11264], BF16))
    IDBF = e(nc.sbuf_tensor("IDBF", [128, 128], BF16))
    DIAG = e(nc.sbuf_tensor("DIAG", [128, 128], BF16))
    ETB = e(nc.sbuf_tensor("ETB", [128, 4 * 128], BF16))
    N1B = e(nc.sbuf_tensor("N1B", [128, 4 * 2064], BF16))
    N1T = e(nc.sbuf_tensor("N1T", [128, 1024], BF16))
    WI = e(nc.sbuf_tensor("WI", [128, 512], I32))
    PS = [e(nc.psum_tensor(f"PS{i}", [128, 512], F32)) for i in range(4)]
    PO = [e(nc.psum_tensor(f"PO{i}", [128, 512], F32)) for i in range(2)]
    PT = e(nc.psum_tensor("PT", [128, 1024], BF16))
    PF = e(nc.psum_tensor("PF", [128, 512], F32))

    PQ, PK, PV, PW = 0, 2048, 4096, 6176
    PN2, PM1S, PM2S = 0, 8192, 10240
    PWO, POF = 0, 8192

    C_SUM, C_SQ, C_M, C_R, C_RM = 0, 16, 32, 48, 64
    C_U, C_U23 = 80, 81
    C_C1, C_T1, C_NT1 = 82, 83, 84
    C_C2, C_T2, C_NT2 = 85, 86, 87
    C_W = 96
    C_RD = 112
    C_S1 = 120
    C_Y = 136
    C_X = 144

    S_LN = 0
    S_WST = 1024
    S_WOS = 1024
    S_QP = 1024
    S_HSC = 1024
    S_UB = 1560
    S_BC = 2100

    RMF = SCR[0:64, 0:TBW]
    X2RB = SCR[:, 2700:2700 + D]

    em = Emitter()
    E = em.emit
    sem_names = ("lc", "lx", "lw", "lq", "lm", "lf", "lr", "ls",
                 "sv", "sa", "st", "sg", "so")
    for name in sem_names:
        em.add_sem(name, e(nc.semaphore(name)))

    for (dst, src) in ((QKB, qkcols), (CW, cwcols), (CBc, cbcols), (CC, ccols),
                       (ID32, cid32), (ONER, conerow), (BROW, bvrow),
                       (IDBF, cidbf), (DIAG, cdiag)):
        E("sync", lambda s, dst=dst, src=src: s.dma_start(out=dst[:], in_=src),
          inc=("lc", 16))
    lc_all = em.now("lc")
    for tt in range(NT):
        E("sync", lambda s, tt=tt: s.dma_start(
            out=X[:, tt * D:(tt + 1) * D], in_=xin[tt * 128:(tt + 1) * 128, :]),
          inc=("lx", 16))
    lx_all = em.now("lx")
    wst = SCR[:, S_WST:S_WST + 3072]
    E("sync", lambda s: s.dma_start(out=wst, in_=wqkv[0]), inc=("lq", 16))

    E("vector", lambda v: v.tensor_scalar(
        COL[:, C_U:C_U + 1], CC[:, 2:3], 0.5, -8388608.0, OP.add, OP.add),
      waits=[("lc", lc_all)], inc=("sv", 1))
    E("vector", lambda v: v.tensor_single_scalar(
        COL[:, C_U23:C_U23 + 1], COL[:, C_U:C_U + 1], MAGIC, OP.add),
      waits=[("sv", em.now("sv"))], inc=("sv", 1))
    E("vector", lambda v: v.memset(
        PAIR.ap()[:, PV + 64:PV + 16 * 130:130], 1.0), inc=("sv", 1))
    E("vector", lambda v: v.memset(
        PAIR.ap()[:, PV + 129:PV + 16 * 130:130], 1.0), inc=("sv", 1))
    E("vector", lambda v: v.memset(S2R[0:1, 0:1], 1.0), inc=("sv", 1))
    E("vector", lambda v: v.memset(S2R[0:1, S + 1:S + 2], 1.0), inc=("sv", 1))
    E("vector", lambda v: v.memset(COL[:, C_S1:C_S1 + 16], 1.0), inc=("sv", 1))
    E("vector", lambda v: v.tensor_single_scalar(
        QKB[:, 0:16:2], QKB[:, 0:16:2], 0.125, OP.mult),
      waits=[("lc", lc_all)], inc=("sv", 1))
    sv_init = em.now("sv")

    def dump_and_stop(bfap, f32ap, waits):
        if bfap is not None:
            E("sync", lambda s: s.dma_start(out=dbg[:, 0:bfap.shape[1]], in_=bfap),
              waits=waits, inc=("so", 16))
        if f32ap is not None:
            E("sync", lambda s: s.dma_start(out=dbgf[:, 0:f32ap.shape[1]], in_=f32ap),
              waits=waits, inc=("so", 16))
        E("gpsimd", None, waits=[("so", em.now("so"))])
        with nc.Block() as block:
            em.replay(block)
        ctx.close()
        return nc

    def ln_stats(tt, waits):
        xt = X[:, tt * D:(tt + 1) * D]
        E("scalar", lambda a, xt=xt, tt=tt: a.activation(
            xt, xt, AF.Copy,
            accum_out=COL[:, C_SUM + tt:C_SUM + tt + 1]),
          waits=waits + [("sa", em.now("sa")), ("sv", em.now("sv"))],
          inc=("sa", 1))
        E("scalar", lambda a, tt=tt: a.activation(
            SCR[:, S_LN:S_LN + D], X[:, tt * D:(tt + 1) * D], AF.Square,
            accum_out=COL[:, C_SQ + tt:C_SQ + tt + 1]),
          inc=("sa", 1))
        return em.now("sa")

    def ln_cols(tt, sa_wait):
        mc = COL[:, C_M + tt:C_M + tt + 1]
        rc = COL[:, C_R + tt:C_R + tt + 1]
        vw = COL[:, C_W:C_W + 1]
        t2 = COL[:, C_W + 1:C_W + 2]
        E("vector", lambda v, mc=mc, tt=tt: v.tensor_single_scalar(
            mc, COL[:, C_SUM + tt:C_SUM + tt + 1], 1.0 / D, OP.mult),
          waits=[("sa", sa_wait), ("sv", em.now("sv"))], inc=("sv", 1))
        E("vector", lambda v, vw=vw, mc=mc: v.tensor_tensor(vw, mc, mc, OP.mult),
          inc=("sv", 1))
        E("vector", lambda v, vw=vw, tt=tt: v.scalar_tensor_tensor(
            vw, COL[:, C_SQ + tt:C_SQ + tt + 1], 1.0 / D, vw, OP.mult, OP.subtract),
          inc=("sv", 1))
        E("vector", lambda v, vw=vw: v.tensor_single_scalar(vw, vw, 1e-5, OP.add),
          inc=("sv", 1))
        sv_v = em.now("sv")
        E("scalar", lambda a, rc=rc, vw=vw: a.activation(rc, vw, AF.Ln),
          waits=[("sv", sv_v), ("sa", em.now("sa"))], inc=("sa", 1))
        E("scalar", lambda a, rc=rc: a.activation(rc, rc, AF.Exp, scale=-0.5),
          inc=("sa", 1))
        sa_z = em.now("sa")
        for _ in range(2):
            E("vector", lambda v, t2=t2, rc=rc: v.tensor_tensor(t2, rc, rc, OP.mult),
              waits=[("sa", sa_z), ("sv", em.now("sv"))], inc=("sv", 1))
            E("vector", lambda v, t2=t2, vw=vw: v.tensor_tensor(t2, t2, vw, OP.mult),
              inc=("sv", 1))
            E("vector", lambda v, t2=t2: v.tensor_scalar(
                t2, t2, -0.5, 1.5, OP.mult, OP.add), inc=("sv", 1))
            E("vector", lambda v, rc=rc, t2=t2: v.tensor_tensor(rc, rc, t2, OP.mult),
              inc=("sv", 1))
        return em.now("sv")

    pt_sv = [0]

    def transpose_to(src_ap, dst_ap, slot, waits=()):
        p = PT[:, slot * 128:(slot + 1) * 128]
        E("tensor", lambda t, p=p, src_ap=src_ap: t.transpose(p, src_ap, IDBF[:]),
          waits=list(waits) + [("sv", pt_sv[0])], inc=("st", 1))
        stv = em.now("st")
        E("vector", lambda v, p=p, dst_ap=dst_ap: v.tensor_copy(dst_ap, p),
          waits=[("st", stv)], inc=("sv", 1))
        pt_sv[0] = em.now("sv")
        return em.now("sv")

    sa_s = [0] * NT
    for tt in range(NT):
        sa_s[tt] = ln_stats(tt, [("lx", lx_all), ("lc", lc_all)])
    for tt in range(NT):
        ln_cols(tt, sa_s[tt])
    sv_ln1c = em.now("sv")
    for tt in range(NT):
        E("vector", lambda v, tt=tt: v.tensor_scalar(
            N1T[:, 0:D], X[:, tt * D:(tt + 1) * D],
            COL[:, C_M + tt:C_M + tt + 1], COL[:, C_R + tt:C_R + tt + 1],
            OP.subtract, OP.mult),
          waits=[("sv", sv_ln1c), ("st", em.now("st"))], inc=("sv", 1))
        sv_ap = em.now("sv")
        for dc in range(DC):
            transpose_to(N1T[:, dc * 128:(dc + 1) * 128],
                         B1[:, dc * 2048 + tt * 128:dc * 2048 + (tt + 1) * 128],
                         dc % 2, waits=[("sv", sv_ap)])
    sa_ln1fm = em.now("sa")
    sv_ln1fm = em.now("sv")

    if STAGE == "ln1":
        return dump_and_stop(B1[:, 0:8192], COL[:, 0:160],
                             [("sa", sa_ln1fm), ("sv", em.now("sv"))])

    wq_used = [0, 0]

    def meanabs_piece(wT, r, cl, acc_c, first):
        acc = COL[:, acc_c:acc_c + 1]
        tmp = COL[:, acc_c + 1:acc_c + 2]
        par = meanabs_piece.n % 2
        meanabs_piece.n += 1
        buf = WQ[:, par * 512:(par + 1) * 512]
        E("sync", lambda s: s.dma_start(
            out=buf, in_=wT[r * 128:(r + 1) * 128, cl * 512:(cl + 1) * 512]),
          waits=[("sv", wq_used[par])], inc=("lm", 16))
        lmv = em.now("lm")
        E("vector", lambda v: v.reduce_sum(
            tmp, buf, AX, apply_absolute_value=True),
          waits=[("lm", lmv)], inc=("sv", 1))
        if first:
            E("vector", lambda v: v.tensor_copy(acc, tmp), inc=("sv", 1))
        else:
            E("vector", lambda v: v.tensor_tensor(acc, acc, tmp, OP.add),
              inc=("sv", 1))
        wq_used[par] = em.now("sv")
    meanabs_piece.n = 0

    def meanabs_finish(nrows, ncols, acc_c, c_c):
        acc = COL[:, acc_c:acc_c + 1]
        sv_acc = em.now("sv")
        E("tensor", lambda t, acc=acc: t.matmul(
            PF[0:1, 0:1], acc, CC[:, 3:4], start=True, stop=True),
          waits=[("sv", sv_acc), ("sa", em.now("sa")), ("st", em.now("st"))],
          inc=("st", 1))
        E("scalar", lambda a, nrows=nrows, ncols=ncols: a.activation(
            COL[0:1, C_W + 4:C_W + 5], PF[0:1, 0:1], AF.Copy,
            scale=1.0 / (nrows * ncols)),
          waits=[("st", em.now("st"))], inc=("sa", 1))
        E("vector", lambda v: v.tensor_single_scalar(
            COL[0:1, C_W + 4:C_W + 5], COL[0:1, C_W + 4:C_W + 5], 1e-5, OP.max),
          waits=[("sa", em.now("sa")), ("sv", em.now("sv"))], inc=("sv", 1))
        E("tensor", lambda t: t.matmul(
            PF[:, 1:2], ONER[:], COL[0:1, C_W + 4:C_W + 5], start=True, stop=True),
          waits=[("sv", em.now("sv"))], inc=("st", 1))
        E("scalar", lambda a, c_c=c_c: a.copy(COL[:, c_c:c_c + 1], PF[:, 1:2]),
          waits=[("st", em.now("st")), ("sa", em.now("sa"))], inc=("sa", 1))
        sa_c = em.now("sa")
        E("vector", lambda v, c_c=c_c: v.tensor_single_scalar(
            COL[:, c_c + 1:c_c + 2], COL[:, c_c:c_c + 1], 0.5, OP.mult),
          waits=[("sa", sa_c), ("sv", em.now("sv"))], inc=("sv", 1))
        E("vector", lambda v, c_c=c_c: v.tensor_single_scalar(
            COL[:, c_c + 2:c_c + 3], COL[:, c_c:c_c + 1], -0.5, OP.mult),
          inc=("sv", 1))
        return em.now("sv")


    sa_po_evac = [0, 0]
    sa_ps = [0] * 4
    st_ps = [0] * 4
    et_use = [0] * 4
    st_pair_w = 0
    st_wbf_free = 0
    st_pv_prev = 0
    wbf = PAIR[:, PW:PW + 3072]

    E("vector", lambda v: v.tensor_copy(wbf, wst),
      waits=[("lq", 16), ("sv", sv_init)], inc=("sv", 1))
    sv_wbf_next = em.now("sv")
    E("sync", lambda s: s.dma_start(out=wst, in_=wqkv[1]),
      waits=[("sv", sv_wbf_next)], inc=("lq", 16))
    for c in range(NPAIR):
        sv_wbf = sv_wbf_next
        for which in range(2):
            coloff = 128 * which
            dstoff = PQ if which == 0 else PK
            for tb4 in range(4):
                bi = 2 * which + (tb4 % 2)
                bank = PS[bi]
                for dc in range(DC):
                    E("tensor", lambda t, dc=dc, coloff=coloff, tb4=tb4, bank=bank:
                      t.matmul(
                        bank[:], wbf[:, dc * 384 + coloff:dc * 384 + coloff + 128],
                        B1[:, dc * 2048 + tb4 * 512:dc * 2048 + (tb4 + 1) * 512],
                        start=(dc == 0), stop=(dc == DC - 1)),
                      waits=([("sv", sv_wbf), ("sv", sv_ln1fm),
                              ("sa", sa_ps[bi])] if dc == 0 else []),
                      inc=(("st", 1) if dc == DC - 1 else None))
                st_ps[bi] = em.now("st")
                E("scalar", lambda a, which=which, tb4=tb4, c=c, dstoff=dstoff,
                  bank=bank: a.activation(
                    PAIR[:, dstoff + tb4 * 512:dstoff + (tb4 + 1) * 512],
                    bank[:], AF.Identity,
                    bias=QKB[:, 2 * c + which:2 * c + which + 1],
                    scale=(0.125 if which == 0 else 1.0)),
                  waits=[("st", st_ps[bi])], inc=("sa", 1))
                sa_ps[bi] = em.now("sa")
        for tt in range(NT):
            bi = tt % 2
            bank = PS[bi]
            E("tensor", lambda t, c=c, bank=bank: t.matmul(
                bank[:, 0:128], ONER[:], BROW[:, c * 128:(c + 1) * 128],
                start=True, stop=False),
              waits=[("sa", sa_ps[bi]), ("st", st_pv_prev)])
            for dc in range(DC):
                E("tensor", lambda t, dc=dc, tt=tt, bank=bank: t.matmul(
                    bank[:, 0:128],
                    B1[:, dc * 2048 + tt * 128:dc * 2048 + (tt + 1) * 128],
                    wbf[:, dc * 384 + 256:dc * 384 + 384],
                    start=False, stop=(dc == DC - 1)),
                  inc=(("st", 1) if dc == DC - 1 else None))
            st_ps[bi] = em.now("st")
            E("scalar", lambda a, tt=tt, bank=bank: a.copy(
                PAIR[:, PV + tt * 130:PV + tt * 130 + 64], bank[:, 0:64]),
              waits=[("st", st_ps[bi]), ("sv", sv_init)], inc=("sa", 1))
            E("scalar", lambda a, tt=tt, bank=bank: a.copy(
                PAIR[:, PV + tt * 130 + 65:PV + tt * 130 + 129], bank[:, 64:128]),
              inc=("sa", 1))
            sa_ps[bi] = em.now("sa")
        sa_v = em.now("sa")
        st_wbf_free = em.now("st")
        if c + 1 < NPAIR:
            E("vector", lambda v: v.tensor_copy(wbf, wst),
              waits=[("lq", 16 * (c + 2)), ("st", st_wbf_free)],
              inc=("sv", 1))
            sv_wbf_next = em.now("sv")
            if c + 2 < NPAIR:
                E("sync", lambda s, c=c: s.dma_start(out=wst, in_=wqkv[c + 2]),
                  waits=[("sv", sv_wbf_next)], inc=("lq", 16))

        if STAGE == "qkv0" and c == 0:
            return dump_and_stop(PAIR[:, 0:8192], None,
                                 [("sa", sa_v), ("sv", em.now("sv"))])

        for qb in range(4):
            po_started = [False, False]
            for qi in range(qb * 4, qb * 4 + 4):
                for ki in ([qi - 1] if qi else []) + [qi]:
                    delta = qi - ki
                    for h2 in range(2):
                        bi = h2
                        bank = PS[bi]
                        E("tensor", lambda t, h2=h2, ki=ki, qi=qi, bank=bank:
                          t.matmul(
                            bank[:, 0:128],
                            PAIR[64 * h2:64 * h2 + 64, PK + ki * 128:PK + ki * 128 + 128],
                            PAIR[64 * h2:64 * h2 + 64, PQ + qi * 128:PQ + qi * 128 + 128],
                            start=True, stop=True),
                          waits=[("sa", sa_v), ("sa", sa_ps[bi])],
                          inc=("st", 1))
                        st_ps[bi] = em.now("st")
                        slot = 2 * h2 + (ki % 2)
                        et = ETB[:, slot * 128:(slot + 1) * 128]
                        E("scalar", lambda a, et=et, bank=bank, delta=delta:
                          a.activation(et, bank[:, 0:128], AF.Exp,
                                       bias=CC[:, delta:delta + 1]),
                          waits=[("st", st_ps[bi]), ("st", et_use[slot])],
                          inc=("sa", 1))
                        sa_ps[bi] = em.now("sa")
                        sa_exp = em.now("sa")
                        if delta == 0:
                            E("vector", lambda v, et=et: v.tensor_tensor(
                                et, et, DIAG[:], OP.mult),
                              waits=[("sa", sa_exp), ("sv", em.now("sv"))],
                              inc=("sv", 1))
                            dep = [("sv", em.now("sv"))]
                        else:
                            dep = [("sa", sa_exp)]
                        po = PO[h2]
                        first = not po_started[h2]
                        po_started[h2] = True
                        last = (qi == qb * 4 + 3) and (ki == qi)
                        E("tensor", lambda t, po=po, qi=qi, ki=ki, h2=h2, et=et,
                          first=first, last=last: t.matmul(
                            po[:, (qi % 4) * 65:(qi % 4) * 65 + 65], et,
                            PAIR[:, PV + ki * 130 + 65 * h2:PV + ki * 130 + 65 * h2 + 65],
                            start=first, stop=last),
                          waits=dep + [("sa", sa_po_evac[h2])],
                          inc=(("st", 1) if last else None))
                        et_use[slot] = em.now("st")
            st_po = em.now("st")
            st_pair_w = st_po
            st_pv_prev = st_po
            for h2 in range(2):
                po = PO[h2]
                E("vector", lambda v, po=po, h2=h2: v.reciprocal(
                    COL[:, C_RD + 4 * h2:C_RD + 4 * h2 + 4], po.ap()[:, 64:260:65]),
                  waits=[("st", st_po), ("sv", em.now("sv"))], inc=("sv", 1))
                sv_rd = em.now("sv")
                for s4 in range(4):
                    qi = qb * 4 + s4
                    hg = 2 * c + h2
                    E("scalar", lambda a, po=po, s4=s4, qi=qi, hg=hg, h2=h2:
                      a.activation(
                        B2[:, qi * 1024 + 64 * hg:qi * 1024 + 64 * hg + 64],
                        po[:, s4 * 65:s4 * 65 + 64], AF.Copy,
                        scale=COL[:, C_RD + 4 * h2 + s4:C_RD + 4 * h2 + s4 + 1]),
                      waits=[("sv", sv_rd), ("sa", em.now("sa"))], inc=("sa", 1))
                sa_po_evac[h2] = em.now("sa")
        for i in range(16):
            k = c * 16 + i
            if k < 64:
                meanabs_piece(w1T, k // 8, k % 8, C_W + 6, k == 0)
            else:
                k -= 64
                meanabs_piece(w2T, k // 2, k % 2, C_W + 8, k == 0)
    sa_attn = em.now("sa")

    if STAGE == "att0":
        return dump_and_stop(B2[:, 0:8192], COL[:, 0:160],
                             [("sa", sa_attn), ("sv", em.now("sv"))])

    h_done_st = [0, 0, 0, 0]
    sa_n1 = [0] * (NT + 1)
    sv_n1 = [0] * (NT + 1)

    def n1_tile(tt):
        w_main = tt // 2
        slot = w_main % 4
        xt = X[:, tt * D:(tt + 1) * D]
        scr = SCR[:, S_LN:S_LN + D]
        sa_st2 = ln_stats(tt, [("sv", sv_x2[tt])])
        ln_cols(tt, sa_st2)
        E("vector", lambda v, scr=scr, xt=xt, tt=tt: v.tensor_scalar(
            scr, xt, COL[:, C_M + tt:C_M + tt + 1], COL[:, C_R + tt:C_R + tt + 1],
            OP.subtract, OP.mult),
          waits=[("sv", em.now("sv"))], inc=("sv", 1))
        rmc = COL[:, C_RM + tt:C_RM + tt + 1]
        s1c = COL[:, C_S1 + tt:C_S1 + tt + 1]
        E("vector", lambda v, rmc=rmc, scr=scr: v.reduce_max(
            rmc, scr, AX, apply_absolute_value=True), inc=("sv", 1))
        E("vector", lambda v, rmc=rmc: v.tensor_single_scalar(
            rmc, rmc, 1e-5, OP.max), inc=("sv", 1))
        E("vector", lambda v, s1c=s1c, rmc=rmc: v.reciprocal(s1c, rmc),
          inc=("sv", 1))
        E("vector", lambda v, s1c=s1c: v.tensor_single_scalar(
            s1c, s1c, 127.0, OP.mult), inc=("sv", 1))
        for hh in range(4):
            E("vector", lambda v, scr=scr, s1c=s1c, hh=hh: v.tensor_scalar(
                WI[:, 0:256], scr[:, hh * 256:(hh + 1) * 256], s1c,
                COL[:, C_U23:C_U23 + 1], OP.mult, OP.add),
              waits=[("sa", em.now("sa")), ("sv", em.now("sv"))], inc=("sv", 1))
            sv_wi = em.now("sv")
            E("scalar", lambda a, hh=hh: a.activation(
                N1T[:, hh * 256:(hh + 1) * 256], WI[:, 0:256], AF.Copy,
                bias=-MAGIC),
              waits=[("sv", sv_wi), ("st", em.now("st")), ("sa", em.now("sa"))],
              inc=("sa", 1))
        win = N1B[:, slot * 2064:slot * 2064 + 2064]
        sa_fin = em.now("sa")
        for fc in range(DC):
            p = PT[:, (fc % 2) * 128:(fc % 2) * 128 + 128]
            E("tensor", lambda t, p=p, fc=fc: t.transpose(
                p, N1T[:, fc * 128:(fc + 1) * 128], IDBF[:]),
              waits=[("sa", sa_fin), ("sv", pt_sv[0]),
                     ("st", h_done_st[slot])],
              inc=("st", 1))
            stv = em.now("st")
            dst0 = fc * 258 + 1 + (tt % 2) * 128
            E("vector", lambda v, p=p, win=win, dst0=dst0: v.tensor_copy(
                win[:, dst0:dst0 + 128], p),
              waits=[("st", stv)], inc=("sv", 1))
            if tt % 2 == 0 and w_main >= 1:
                prv = N1B[:, ((w_main - 1) % 4) * 2064:((w_main - 1) % 4) * 2064 + 2064]
                E("vector", lambda v, p=p, prv=prv, fc=fc: v.tensor_copy(
                    prv[:, fc * 258 + 257:fc * 258 + 258], p[:, 0:1]),
                  inc=("sv", 1))
            pt_sv[0] = em.now("sv")
        if tt == 0:
            for fc in range(DC):
                E("scalar", lambda a, win=win, fc=fc: a.copy(
                    win[:, fc * 258:fc * 258 + 1], CC[:, 4:5]), inc=("sa", 1))
        if tt == NT - 1:
            for fc in range(DC):
                E("scalar", lambda a, win=win, fc=fc: a.copy(
                    win[:, fc * 258 + 257:fc * 258 + 258], CC[:, 4:5]),
                  inc=("sa", 1))
        E("tensor", lambda t, tt=tt: t.transpose(
            PF[0:128, 0:1].cast(F32) if False else PF[0:1, 0:128],
            COL[:, C_S1 + tt:C_S1 + tt + 1], ID32[:]),
          waits=[("sa", em.now("sa")), ("sv", em.now("sv"))], inc=("st", 1)) \
            if False else None
        sa_n1[tt] = em.now("sa")
        sv_n1[tt] = em.now("sv")


    meanabs_finish(D, CH, C_W + 6, C_C1)
    sv_c1 = em.now("sv")
    meanabs_finish(CH, D, C_W + 8, C_C2)
    sv_c2 = em.now("sv")

    pos = SCR[:, S_QP:S_QP + 512]

    def wquant_piece(wT, r, cl, c_c, sv_c, store):
        par = wquant_piece.n % 2
        wquant_piece.n += 1
        buf = WQ[:, par * 512:(par + 1) * 512]
        E("sync", lambda s: s.dma_start(
            out=buf, in_=wT[r * 128:(r + 1) * 128, cl * 512:(cl + 1) * 512]),
          waits=[("sv", wq_used[par])], inc=("lm", 16))
        lmv = em.now("lm")
        E("vector", lambda v: v.tensor_single_scalar(
            pos, buf, COL[:, c_c + 1:c_c + 2], OP.is_gt),
          waits=[("lm", lmv), ("sv", sv_c)], inc=("sv", 1))
        E("vector", lambda v: v.scalar_tensor_tensor(
            pos, buf, COL[:, c_c + 2:c_c + 3], pos, OP.is_lt, OP.subtract),
          inc=("sv", 1))
        wq_used[par] = em.now("sv")
        store(r, cl)
    wquant_piece.n = 0

    def store_m1(r, cl):
        col0 = cl * 512
        if col0 + 512 <= 2048:
            E("vector", lambda v: v.tensor_copy(
                B1[:, r * 2048 + col0:r * 2048 + col0 + 512], pos),
              waits=[("st", em.now("st"))], inc=("sv", 1))
        else:
            E("vector", lambda v: v.tensor_copy(ETB[:, 0:512], pos),
              waits=[("lf", em.now("lf")), ("st", em.now("st"))],
              inc=("sv", 1))
            svv = em.now("sv")
            for i in range(4):
                cb = (col0 - 2048) // 128 + i
                E("scalar", lambda a, cb=cb, i=i, r=r: a.dma_start(
                    out=m1h[cb, :, r, :],
                    in_=ETB[:, i * 128:(i + 1) * 128]),
                  waits=[("sv", svv)], inc=("lf", 16))

    def store_m2(r, cl):
        cb, j0 = r, cl * 512
        if cb < 16:
            E("vector", lambda v: v.tensor_copy(
                B2[:, cb * 1024 + j0:cb * 1024 + j0 + 512], pos),
              waits=[("st", em.now("st"))], inc=("sv", 1))
        else:
            E("vector", lambda v, j0=j0: v.tensor_copy(
                PAIR[:, PM2S + j0:PM2S + j0 + 512], pos),
              waits=[("lf", em.now("lf"))], inc=("sv", 1))
            svv = em.now("sv")
            if j0 == 512:
                E("scalar", lambda a, cb=cb: a.dma_start(
                    out=m2h[cb - 16, :, :], in_=PAIR[:, PM2S:PM2S + 1024]),
                  waits=[("sv", svv)], inc=("lf", 16))

    def s1_rows(tt0):
        E("tensor", lambda t: t.transpose(PF[0:16, 0:128], COL[:, C_S1:C_S1 + 16],
                                          ID32[:]),
          waits=[("sa", em.now("sa")), ("sv", em.now("sv")),
                 ("lr", em.now("lr"))],
          inc=("st", 1))
        stv = em.now("st")
        E("scalar", lambda a: a.copy(WQ[0:16, 0:128], PF[0:16, 0:128]),
          waits=[("st", stv)], inc=("sa", 1))
        sav = em.now("sa")
        for tt in (tt0, tt0 + 1):
            E("sync", lambda s, tt=tt: s.dma_start(
                out=S2R[0:1, 1 + tt * 128:1 + (tt + 1) * 128],
                in_=WQ[tt % 16:tt % 16 + 1, 0:128]),
              waits=[("sa", sav)], inc=("lr", 16))
        return ("lr", em.now("lr"))

    E("sync", lambda s: s.dma_start(out=BROW[:], in_=borow),
      waits=[("st", st_pair_w)], inc=("lw", 16))
    lw_bo = em.now("lw")
    for oc in range(DC):
        buf = SCR[:, S_WOS + (oc % 2) * 1024:S_WOS + (oc % 2) * 1024 + 1024]
        E("sync", lambda s, oc=oc, buf=buf: s.dma_start(
            out=buf, in_=woutT[oc * 128:(oc + 1) * 128, :]),
          waits=[("sv", em.now("sv"))], inc=("lw", 16))
        E("vector", lambda v, oc=oc, buf=buf: v.tensor_copy(
            PAIR[:, PWO + oc * 1024:PWO + (oc + 1) * 1024], buf),
          waits=[("lw", em.now("lw")), ("st", st_pair_w), ("sv", em.now("sv"))],
          inc=("sv", 1))
    sv_wo = em.now("sv")

    sv_x2 = [0] * NT
    sv_ps_free = [em.now("sv")] * 4
    of_done = [0, 0]
    st_of_used = [0, 0]

    def emit_of_transposes(tt):
        par = tt % 2
        dv = 0
        for oc in range(DC):
            dv = transpose_to(
                B2[:, tt * 1024 + oc * 128:tt * 1024 + (oc + 1) * 128],
                PAIR[:, POF + par * 1024 + oc * 128:POF + par * 1024 + (oc + 1) * 128],
                oc % 2,
                waits=[("sa", sa_attn), ("st", st_of_used[par])])
        of_done[par] = dv

    emit_of_transposes(0)
    for tt in range(NT):
        par = tt % 2
        if tt + 1 < NT:
            emit_of_transposes(tt + 1)
        for jb in range(2):
            bi = jb
            bank = PS[bi]
            E("tensor", lambda t, jb=jb, bank=bank: t.matmul(
                bank[:], ONER[:], BROW[:, jb * 512:(jb + 1) * 512],
                start=True, stop=False),
              waits=[("sv", of_done[par]), ("sv", sv_wo), ("sv", sv_ps_free[bi]),
                     ("lw", lw_bo)])
            for oc in range(DC):
                E("tensor", lambda t, oc=oc, jb=jb, bank=bank, par=par: t.matmul(
                    bank[:],
                    PAIR[:, POF + par * 1024 + oc * 128:POF + par * 1024 + (oc + 1) * 128],
                    PAIR[:, PWO + oc * 1024 + jb * 512:PWO + oc * 1024 + jb * 512 + 512],
                    start=False, stop=(oc == DC - 1)),
                  inc=(("st", 1) if oc == DC - 1 else None))
            if jb == 1:
                st_of_used[par] = em.now("st")
            st_ps[bi] = em.now("st")
            E("vector", lambda v, tt=tt, jb=jb, bank=bank: v.tensor_tensor(
                X[:, tt * D + jb * 512:tt * D + (jb + 1) * 512],
                X[:, tt * D + jb * 512:tt * D + (jb + 1) * 512],
                bank[:], OP.add),
              waits=[("st", st_ps[bi])], inc=("sv", 1))
            sv_ps_free[bi] = em.now("sv")
        sv_x2[tt] = em.now("sv")
        E("sync", lambda s, tt=tt: s.dma_start(
            out=out[tt * 128:(tt + 1) * 128, :], in_=X[:, tt * D:(tt + 1) * D]),
          waits=[("sv", sv_x2[tt])], inc=("so", 16))
        if tt < 8:
            n1_tile(tt)
        for i in range(4):
            k = tt * 4 + i
            wquant_piece(w1T, k // 8, k % 8, C_C1, sv_c1, store_m1)
        for cl in range(2):
            wquant_piece(w2T, tt, cl, C_C2, sv_c2, store_m2)
            wquant_piece(w2T, 16 + tt, cl, C_C2, sv_c2, store_m2)
    so_x2 = em.now("so")
    sv_outproj = em.now("sv")

    if STAGE == "x2":
        E("gpsimd", None, waits=[("so", em.now("so"))])
        with nc.Block() as block:
            em.replay(block)
        ctx.close()
        return nc


    sv_m1 = sv_m2 = em.now("sv")
    lf_wq = em.now("lf")

    sa_s1r = [("lr", 0)] * NTB
    for w in range(4):
        sa_s1r[w] = s1_rows(2 * w)

    sv_hs_free, sg_hs_free = em.now("sv"), em.now("sg")
    sv_w2_prev = em.now("sv")
    st_w2 = [em.now("st")] * 4
    st_m1buf = [em.now("st")] * 2
    st_m2buf = [em.now("st")] * 2
    st_ysc = 0
    sv_seed_ok = 0
    HS = 0
    sb1 = SCR[:, S_BC:S_BC + 258]
    sb2 = SCR[:, S_BC + 300:S_BC + 300 + TBW]
    lr_rb = 0

    col0_sa = [0] * (NTB + 1)
    for tb in range(NTB):
        slot = tb % 4
        win = N1B[:, slot * 2064:slot * 2064 + 2064]
        if tb + 1 < NTB:
            nslot = (tb + 1) % 4
            nwin = N1B[:, nslot * 2064:nslot * 2064 + 2064]
            E("scalar", lambda a, win=win, nwin=nwin: a.copy(
                nwin.rearrange("p (f c) -> p f c", f=8, c=258)[:, :, 0:1],
                win.rearrange("p (f c) -> p f c", f=8, c=258)[:, :, 256:257]),
              waits=[("sa", sa_n1[2 * tb + 1]), ("sv", sv_n1[2 * tb + 1]),
                     ("sa", sa_n1[min(2 * tb + 3, NT - 1)]),
                     ("sv", sv_n1[min(2 * tb + 3, NT - 1)]),
                     ("st", h_done_st[nslot]), ("sa", em.now("sa"))],
              inc=("sa", 1))
            col0_sa[tb + 1] = em.now("sa")
        win_ready = [("sa", sa_n1[min(2 * tb + 2, NT - 1)]),
                     ("sv", sv_n1[min(2 * tb + 2, NT - 1)]),
                     ("sa", sa_n1[2 * tb + 1]), ("sv", sv_n1[2 * tb + 1]),
                     ("sa", col0_sa[tb]), sa_s1r[tb]]
        E("tensor", lambda t, tb=tb: t.matmul(
            PF[:, 0:258], ONER[:], S2R[0:1, tb * TBW:tb * TBW + 258],
            start=True, stop=True),
          waits=win_ready + [("sa", em.now("sa")), ("sv", em.now("sv")),
                             ("st", em.now("st"))],
          inc=("st", 1))
        stv = em.now("st")
        E("vector", lambda v: v.reciprocal(sb1, PF[:, 0:258]),
          waits=[("st", stv), ("sv", em.now("sv"))], inc=("sv", 1))
        E("vector", lambda v: v.tensor_scalar(
            sb1, sb1, COL[:, C_C1:C_C1 + 1], -1.0, OP.mult, OP.mult),
          waits=[("sv", sv_m1)], inc=("sv", 1))
        sv_sb1 = em.now("sv")

        sv_run = 0
        sv_hsc = [0, 0]
        cb_order = [x for pr in zip(range(16), range(16, 32)) for x in pr]
        for cbi, cb in enumerate(cb_order):
            par = cbi % 2
            hbank = PO[par]
            if cb >= 16:
                mbuf = PAIR[:, PM1S + par * 1024:PM1S + par * 1024 + 1024]
                E("sync", lambda s, cb=cb, mbuf=mbuf: s.dma_start(
                    out=mbuf, in_=m1h[cb - 16].rearrange("p f c -> p (f c)")),
                  waits=[("lf", lf_wq), ("sv", sv_m1), ("st", st_m1buf[par])],
                  inc=("ls", 16))
                ls_m1 = em.now("ls")
            for fc in range(DC):
                if cb < 16:
                    lhsT = B1[:, fc * 2048 + cb * 128:fc * 2048 + (cb + 1) * 128]
                    lw = [("sv", sv_m1)] if fc == 0 else []
                else:
                    lhsT = PAIR[:, PM1S + par * 1024 + fc * 128:
                                PM1S + par * 1024 + (fc + 1) * 128]
                    lw = [("ls", ls_m1)] if fc == 0 else []
                E("tensor", lambda t, lhsT=lhsT, fc=fc, hbank=hbank, win=win:
                  t.matmul(hbank[:, 0:258], lhsT, win[:, fc * 258:(fc + 1) * 258],
                           start=(fc == 0), stop=(fc == DC - 1)),
                  waits=(lw + win_ready +
                         [("sv", sv_hsc[par]), ("sa", sa_po_evac[par])])
                  if fc == 0 else lw,
                  inc=(("st", 1) if fc == DC - 1 else None))
            st_h = em.now("st")
            if cb >= 16:
                st_m1buf[par] = st_h
            h_done_st[slot] = st_h
            hsc = SCR[:, S_HSC + par * 260:S_HSC + par * 260 + 258]
            E("vector", lambda v, hsc=hsc, hbank=hbank: v.tensor_tensor(
                hsc, hbank[:, 0:258], sb1, OP.mult),
              waits=[("st", st_h), ("sv", sv_sb1), ("sg", em.now("sg")),
                     ("sv", em.now("sv"))],
              inc=("sv", 1))
            sv_hsc[par] = em.now("sv")
            ub = SCR[:, S_UB + par * 260:S_UB + par * 260 + 256]
            E("vector", lambda v, ub=ub, hsc=hsc, cb=cb: v.tensor_single_scalar(
                ub, hsc[:, 0:256], CW[:, 3 * cb:3 * cb + 1], OP.mult),
              inc=("sv", 1))
            E("vector", lambda v, ub=ub, hsc=hsc, cb=cb: v.scalar_tensor_tensor(
                ub, hsc[:, 1:257], CW[:, 3 * cb + 1:3 * cb + 2], ub,
                OP.mult, OP.add),
              inc=("sv", 1))
            E("vector", lambda v, ub=ub, hsc=hsc, cb=cb: v.scalar_tensor_tensor(
                ub, hsc[:, 2:258], CW[:, 3 * cb + 2:3 * cb + 3], ub,
                OP.mult, OP.add),
              inc=("sv", 1))
            sv_conv = em.now("sv")
            hss = X[:, HS + cb * TBW:HS + (cb + 1) * TBW]
            E("scalar", lambda a, hss=hss, ub=ub, cb=cb: a.activation(
                hss, ub, AF.Silu, bias=CBc[:, cb:cb + 1]),
              waits=[("sv", sv_conv), ("sv", sv_hs_free), ("sa", em.now("sa"))],
              inc=("sa", 1))
            sa_silu = em.now("sa")
        hsview = X[:, 0:NCB * TBW].rearrange("p (cb t) -> p t cb", cb=NCB, t=TBW)
        E("vector", lambda v, hsview=hsview: v.reduce_max(
            RMX[:, 0:TBW], hsview, AX, apply_absolute_value=True),
          waits=[("sa", sa_silu), ("st", st_ysc), ("sv", em.now("sv"))],
          inc=("sv", 1))
        sv_run = em.now("sv")
        for half in range(2):
            E("tensor", lambda t, half=half: t.transpose(
                PF[:, half * 128:(half + 1) * 128],
                RMX[:, half * 128:(half + 1) * 128], ID32[:]),
              waits=[("sv", sv_run), ("st", em.now("st"))],
              inc=("st", 1))
        st_tr = em.now("st")
        rmc = COL[:, C_X:C_X + 2]
        for half in range(2):
            E("vector", lambda v, half=half: v.reduce_max(
                rmc[:, half:half + 1], PF[:, half * 128:(half + 1) * 128], AX),
              waits=[("st", st_tr)], inc=("sv", 1))
        E("vector", lambda v, rmc=rmc: v.tensor_single_scalar(
            rmc, rmc, 1e-5, OP.max), inc=("sv", 1))
        sv_rm = em.now("sv")
        ysc = COL[:, C_Y + (tb % 2) * 2:C_Y + (tb % 2) * 2 + 2]
        E("vector", lambda v, ysc=ysc, rmc=rmc: v.tensor_scalar(
            ysc, rmc, COL[:, C_C2:C_C2 + 1], -1.0 / 127.0, OP.mult, OP.mult),
          waits=[("sv", sv_m2)], inc=("sv", 1))
        sv_ysc = em.now("sv")
        s2c = COL[:, C_X + 2:C_X + 4]
        E("vector", lambda v, s2c=s2c, rmc=rmc: v.reciprocal(s2c, rmc),
          inc=("sv", 1))
        E("vector", lambda v, s2c=s2c: v.tensor_single_scalar(
            s2c, s2c, 127.0, OP.mult), inc=("sv", 1))
        sv_s2 = em.now("sv")
        for half in range(2):
            E("tensor", lambda t, half=half, s2c=s2c: t.matmul(
                PF[0:1, 256 + half * 128:256 + (half + 1) * 128],
                s2c[:, half:half + 1], ID32[:], start=True, stop=True),
              waits=[("sv", sv_s2)], inc=("st", 1))
        stv = em.now("st")
        row = WQ[0:1, 256:256 + TBW]
        E("scalar", lambda a, row=row: a.copy(row, PF[0:1, 256:512]),
          waits=[("st", stv)], inc=("sa", 1))
        sa_row2 = em.now("sa")
        E("tensor", lambda t, row=row: t.matmul(
            PF[:, 0:TBW], ONER[:], row, start=True, stop=True),
          waits=[("sa", sa_row2), ("sv", em.now("sv"))],
          inc=("st", 1))
        stv = em.now("st")
        st_ysc = stv
        E("scalar", lambda a: a.copy(sb2, PF[:, 0:TBW]),
          waits=[("st", stv)], inc=("sa", 1))
        sa_s2b = em.now("sa")
        sa_n2cb = [0] * NCB
        for cb in cb_order:
            hss = X[:, HS + cb * TBW:HS + (cb + 1) * TBW]
            E("vector", lambda v, hss=hss: v.tensor_tensor(hss, hss, sb2, OP.mult),
              waits=[("sa", sa_s2b), ("sv", em.now("sv"))], inc=("sv", 1))
            E("vector", lambda v, hss=hss: v.tensor_scalar(
                WI[:, 256:256 + TBW], hss, 1.0, COL[:, C_U23:C_U23 + 1],
                OP.mult, OP.add),
              waits=[("sa", em.now("sa"))], inc=("sv", 1))
            sv_wiq = em.now("sv")
            E("scalar", lambda a, cb=cb: a.activation(
                PAIR[:, PN2 + cb * TBW:PN2 + (cb + 1) * TBW], WI[:, 256:256 + TBW],
                AF.Copy, bias=-MAGIC),
              waits=[("sv", sv_wiq), ("sa", em.now("sa"))], inc=("sa", 1))
            sa_n2cb[cb] = em.now("sa")
        sa_n2 = em.now("sa")
        sv_hs_free = em.now("sv")

        if STAGE == "ffn0" and tb == DBG_TB:
            E("sync", lambda s: s.dma_start(out=dbgh[:, 0:8192], in_=X[:, 0:8192]),
              waits=[("sv", em.now("sv")), ("sa", em.now("sa"))], inc=("so", 16))
            E("sync", lambda s: s.dma_start(out=dbgf[0:1, 0:512], in_=S2R[0:1, 0:512]),
              waits=[("sv", em.now("sv"))], inc=("so", 16))
            E("sync", lambda s: s.dma_start(out=dbgf[1:2, 0:160], in_=COL[0:1, 0:160]),
              inc=("so", 16))
            E("sync", lambda s: s.dma_start(out=dbgf[2:128, 0:160],
                                            in_=COL[1:127, 0:160]),
              inc=("so", 16))
            E("sync", lambda s: s.dma_start(out=dbgf[0:128, 160:418],
                                            in_=SCR[:, 2100:2358]),
              inc=("so", 16))
            return dump_and_stop(N1B[:, (tb % 4) * 2064:(tb % 4) * 2064 + 2064], None,
                                 [("sa", sa_n2), ("sv", em.now("sv"))])

        for cbi, cb in enumerate(cb_order):
            if cb < 16:
                rhs = B2[:, cb * 1024:(cb + 1) * 1024]
                rw = [("sv", sv_m2)] if cb == 0 else []
            else:
                mbuf = PAIR[:, PM2S:PM2S + 1024]
                E("sync", lambda s, cb=cb, mbuf=mbuf: s.dma_start(
                    out=mbuf, in_=m2h[cb - 16]),
                  waits=[("lf", lf_wq), ("sv", sv_m2), ("st", st_m2buf[0])],
                  inc=("ls", 16))
                rhs = mbuf
                rw = [("ls", em.now("ls"))]
            for ht in range(2):
                for jb in range(2):
                    bank = PS[2 * ht + jb]
                    E("tensor", lambda t, cb=cb, ht=ht, jb=jb, bank=bank, rhs=rhs,
                      cbi=cbi:
                      t.matmul(
                        bank[:],
                        PAIR[:, PN2 + cb * TBW + ht * 128:PN2 + cb * TBW + ht * 128 + 128],
                        rhs[:, jb * 512:(jb + 1) * 512],
                        start=(cbi == 0), stop=(cbi == NCB - 1)),
                      waits=(rw + [("sa", sa_n2cb[cb]), ("sv", sv_w2_prev)])
                      if (ht == 0 and jb == 0) else [],
                      inc=(("st", 1) if (ht == 1 and jb == 1) else None))
            if cb >= 16:
                st_m2buf[0] = em.now("st")
        st_w2e = em.now("st")
        if STAGE == "w2d" and tb == DBG_TB:
            for bi in range(4):
                E("vector", lambda v, bi=bi: v.tensor_copy(
                    SCR[:, bi * 512:(bi + 1) * 512], PS[bi][:]),
                  waits=[("st", st_w2e), ("sv", em.now("sv"))], inc=("sv", 1))
            svv = em.now("sv")
            E("sync", lambda s: s.dma_start(out=dbgh[:, 0:2048],
                                            in_=SCR[:, 0:2048]),
              waits=[("sv", svv)], inc=("so", 16))
            E("sync", lambda s: s.dma_start(out=dbg[:, 0:8192],
                                            in_=PAIR[:, PN2:PN2 + 8192]),
              waits=[("sa", sa_n2)], inc=("so", 16))
            E("sync", lambda s: s.dma_start(out=dbgf[0:1, 0:160],
                                            in_=COL[0:1, 0:160]),
              inc=("so", 16))
            E("gpsimd", None, waits=[("so", em.now("so"))])
            with nc.Block() as block:
                em.replay(block)
            ctx.close()
            return nc
        for ht in range(2):
            tt = 2 * tb + ht
            E("sync", lambda s, tt=tt: s.dma_start(
                out=X2RB[:, 0:D], in_=out[tt * 128:(tt + 1) * 128, :]),
              waits=[("so", em.now("so")), ("sv", em.now("sv"))],
              inc=("lr", 16))
            lr_rb = em.now("lr")
            for jb in range(2):
                bank = PS[2 * ht + jb]
                ysc1 = COL[:, C_Y + (tb % 2) * 2 + ht:C_Y + (tb % 2) * 2 + ht + 1]
                E("vector", lambda v, bank=bank, jb=jb, ysc1=ysc1:
                  v.scalar_tensor_tensor(
                    X2RB[:, jb * 512:(jb + 1) * 512],
                    bank[:], ysc1,
                    X2RB[:, jb * 512:(jb + 1) * 512],
                    OP.mult, OP.add),
                  waits=[("st", st_w2e), ("lr", lr_rb), ("sv", sv_ysc),
                         ("sv", em.now("sv"))],
                  inc=("sv", 1))
            sv_y = em.now("sv")
            E("sync", lambda s, tt=tt: s.dma_start(
                out=out[tt * 128:(tt + 1) * 128, :], in_=X2RB[:, 0:D]),
              waits=[("sv", sv_y)], inc=("so", 16))
        sv_w2_prev = em.now("sv")

        if 2 * tb + 8 < NT:
            n1_tile(2 * tb + 8)
            n1_tile(2 * tb + 9)
            sa_s1r[tb + 4] = s1_rows(2 * tb + 8)

    E("gpsimd", None, waits=[("so", em.now("so"))])

    with nc.Block() as block:
        em.replay(block)
    ctx.close()
    return nc


_BF = ml_dtypes.bfloat16


def kernel(L0_attn_x, L0_attn_w_qkv, L0_attn_b_qkv, L0_attn_w_out, L0_attn_b_out,
           L0_ffn_w1, L0_ffn_conv_w, L0_ffn_conv_b, L0_ffn_w2):
    x = np.ascontiguousarray(np.asarray(L0_attn_x, dtype=np.float32))
    w_qkv = np.asarray(L0_attn_w_qkv, dtype=np.float32)
    b_qkv = np.asarray(L0_attn_b_qkv, dtype=np.float32)
    w_out = np.asarray(L0_attn_w_out, dtype=np.float32)
    b_out = np.asarray(L0_attn_b_out, dtype=np.float32)
    w1 = np.asarray(L0_ffn_w1, dtype=np.float32)
    conv_w = np.asarray(L0_ffn_conv_w, dtype=np.float32)
    conv_b = np.asarray(L0_ffn_conv_b, dtype=np.float32)
    w2 = np.asarray(L0_ffn_w2, dtype=np.float32)

    wqkvT = w_qkv.T
    wqkv_pair = np.ascontiguousarray(np.stack([
        np.concatenate([wqkvT[:, 128 * c:128 * c + 128],
                        wqkvT[:, 1024 + 128 * c:1024 + 128 * c + 128],
                        wqkvT[:, 2048 + 128 * c:2048 + 128 * c + 128]],
                       axis=1).reshape(8, 128, 384).transpose(1, 0, 2).reshape(128, 3072)
        for c in range(NPAIR)]))
    qkcols = np.zeros((128, 16), dtype=np.float32)
    for c in range(NPAIR):
        qkcols[:, 2 * c] = b_qkv[128 * c:128 * c + 128]
        qkcols[:, 2 * c + 1] = b_qkv[1024 + 128 * c:1024 + 128 * c + 128]
    bvrow = np.ascontiguousarray(b_qkv[2048:3072])[None, :]
    borow = np.ascontiguousarray(b_out)[None, :]
    woutT_a = np.ascontiguousarray(w_out.T)
    w1T_a = np.ascontiguousarray(w1.T)
    w2T_a = np.ascontiguousarray(w2.T)
    cw = conv_w.reshape(CH, 3)
    cwcols = np.zeros((128, 96), dtype=np.float32)
    cbcols = np.zeros((128, 32), dtype=np.float32)
    for cb in range(NCB):
        for k in range(3):
            cwcols[:, 3 * cb + k] = cw[cb * 128:(cb + 1) * 128, k]
        cbcols[:, cb] = conv_b[cb * 128:(cb + 1) * 128]
    kp = np.arange(128, dtype=np.float32)[:, None]
    qf = np.arange(128, dtype=np.float32)[None, :]
    ccols = np.zeros((128, 8), dtype=np.float32)
    ccols[:, 0:1] = kp - 64.0
    ccols[:, 1:2] = kp - 192.0
    ccols[:, 2] = 8388608.0
    ccols[:, 3] = 1.0
    cdiag = (kp <= qf).astype(_BF)
    cid32 = np.eye(128, dtype=np.float32)
    cidbf = np.eye(128, dtype=np.float32).astype(_BF)
    conerow = np.ones((1, 128), dtype=np.float32)

    nc = build_program()
    in_maps = [{
        "xin": np.ascontiguousarray(x[b]),
        "wqkv": wqkv_pair, "woutT": woutT_a, "w1T": w1T_a, "w2T": w2T_a,
        "qkcols": qkcols, "bvrow": bvrow, "borow": borow,
        "cwcols": cwcols, "cbcols": cbcols,
        "cid32": cid32, "cidbf": cidbf, "cdiag": cdiag, "ccols": ccols,
        "conerow": conerow,
    } for b in range(8)]
    res = run_bass_kernel_spmd(nc, in_maps, core_ids=list(range(8)))
    return np.stack([r["out"] for r in res.results])


# version 12
# speedup vs baseline: 1.0854x; 1.0261x over previous
import numpy as np
import ml_dtypes
import concourse.bass as bass
import concourse.mybir as mybir
from concourse.bass_utils import run_bass_kernel_spmd

F32 = mybir.dt.float32
BF16 = mybir.dt.bfloat16
I32 = mybir.dt.int32
AX = mybir.AxisListType.X
OP = mybir.AluOpType
AF = mybir.ActivationFunctionType

S = 2048
D = 1024
NT = 16
DC = 8
NPAIR = 8
CH = 4096
NCB = 32
NTB = 8
TBW = 256
MAGIC = 12582912.0
STAGE = "full"
DBG_NOVBIAS = False
DBG_TB = 0


class Emitter:
    ENGINES = ("sync", "vector", "scalar", "tensor", "gpsimd")
    AUTO = {"vector": "sv", "scalar": "sa", "gpsimd": "sg"}

    def __init__(self):
        self.events = {e: [] for e in self.ENGINES}
        self.counts = {}
        self.handles = {}
        self.last_wait = {e: {} for e in self.ENGINES}

    def add_sem(self, name, handle):
        self.handles[name] = handle
        self.counts[name] = 0

    def emit(self, eng, fn=None, waits=(), inc=None):
        auto = self.AUTO.get(eng)
        if fn is not None and auto is not None:
            waits = list(waits) + [(auto, self.counts[auto])]
            if inc is None:
                inc = (auto, 1)
        evs = []
        for s, v in waits:
            if v > 0 and self.last_wait[eng].get(s, -1) < v:
                evs.append((s, v))
                self.last_wait[eng][s] = v
        self.events[eng].append((evs, fn, inc))
        if inc is not None:
            self.counts[inc[0]] += inc[1]

    def now(self, sem):
        return self.counts[sem]

    def replay(self, block):
        handles = self.handles
        for engname in self.ENGINES:
            evlist = self.events[engname]

            def body(eng, evlist=evlist):
                for waits, fn, inc in evlist:
                    for (s, v) in waits:
                        eng.wait_ge(handles[s], v)
                    if fn is not None:
                        inst = fn(eng)
                        if inc is not None:
                            inst.then_inc(handles[inc[0]], inc[1])
            getattr(block, engname)(body)


def build_program():
    nc = bass.Bass("TRN2", target_bir_lowering=False,
                   detect_race_conditions=False)

    xin = nc.dram_tensor("xin", [S, D], F32, kind="ExternalInput").ap()
    wqkv = nc.dram_tensor("wqkv", [NPAIR, 128, 3072], F32, kind="ExternalInput").ap()
    woutT = nc.dram_tensor("woutT", [D, D], F32, kind="ExternalInput").ap()
    w1T = nc.dram_tensor("w1T", [D, CH], F32, kind="ExternalInput").ap()
    w2T = nc.dram_tensor("w2T", [CH, D], F32, kind="ExternalInput").ap()
    qkcols = nc.dram_tensor("qkcols", [128, 16], F32, kind="ExternalInput").ap()
    bvrow = nc.dram_tensor("bvrow", [1, D], F32, kind="ExternalInput").ap()
    borow = nc.dram_tensor("borow", [1, D], F32, kind="ExternalInput").ap()
    cwcols = nc.dram_tensor("cwcols", [128, 96], F32, kind="ExternalInput").ap()
    cbcols = nc.dram_tensor("cbcols", [128, 32], F32, kind="ExternalInput").ap()
    cid32 = nc.dram_tensor("cid32", [128, 128], F32, kind="ExternalInput").ap()
    cidbf = nc.dram_tensor("cidbf", [128, 128], BF16, kind="ExternalInput").ap()
    cdiag = nc.dram_tensor("cdiag", [128, 128], BF16, kind="ExternalInput").ap()
    ccols = nc.dram_tensor("ccols", [128, 8], F32, kind="ExternalInput").ap()
    conerow = nc.dram_tensor("conerow", [1, 128], F32, kind="ExternalInput").ap()
    out = nc.dram_tensor("out", [S, D], F32, kind="ExternalOutput").ap()
    dbg = nc.dram_tensor("dbg", [128, 8192], BF16, kind="ExternalOutput").ap()
    dbgf = nc.dram_tensor("dbgf", [128, 512], F32, kind="ExternalOutput").ap()
    dbgh = nc.dram_tensor("dbgh", [128, 8192], F32, kind="ExternalOutput").ap()
    m1h = nc.dram_tensor("m1h", [16, 128, 8, 128], BF16, kind="Internal").ap()
    m2h = nc.dram_tensor("m2h", [16, 128, D], BF16, kind="Internal").ap()

    from contextlib import ExitStack
    ctx = ExitStack()
    e = ctx.enter_context

    X = e(nc.sbuf_tensor("X", [128, NT * D], F32))
    SCR = e(nc.sbuf_tensor("SCR", [128, 4096], F32))
    WQ = e(nc.sbuf_tensor("WQ", [128, 1024], F32))
    COL = e(nc.sbuf_tensor("COL", [128, 160], F32))
    QKB = e(nc.sbuf_tensor("QKB", [128, 16], F32))
    CW = e(nc.sbuf_tensor("CW", [128, 96], F32))
    CBc = e(nc.sbuf_tensor("CBc", [128, 32], F32))
    CC = e(nc.sbuf_tensor("CC", [128, 8], F32))
    ID32 = e(nc.sbuf_tensor("ID32", [128, 128], F32))
    ONER = e(nc.sbuf_tensor("ONER", [1, 128], F32))
    BROW = e(nc.sbuf_tensor("BROW", [1, D], F32))
    S2R = e(nc.sbuf_tensor("S2R", [1, S + 2], F32))
    RMX = e(nc.sbuf_tensor("RMX", [128, TBW], F32))
    B1 = e(nc.sbuf_tensor("B1", [128, 16384], BF16))
    B2 = e(nc.sbuf_tensor("B2", [128, 16384], BF16))
    PAIR = e(nc.sbuf_tensor("PAIR", [128, 11264], BF16))
    IDBF = e(nc.sbuf_tensor("IDBF", [128, 128], BF16))
    DIAG = e(nc.sbuf_tensor("DIAG", [128, 128], BF16))
    ETB = e(nc.sbuf_tensor("ETB", [128, 4 * 128], BF16))
    N1B = e(nc.sbuf_tensor("N1B", [128, 4 * 2064], BF16))
    N1T = e(nc.sbuf_tensor("N1T", [128, 1024], BF16))
    WI = e(nc.sbuf_tensor("WI", [128, 512], I32))
    PS = [e(nc.psum_tensor(f"PS{i}", [128, 512], F32)) for i in range(4)]
    PO = [e(nc.psum_tensor(f"PO{i}", [128, 512], F32)) for i in range(2)]
    PT = e(nc.psum_tensor("PT", [128, 1024], BF16))
    PF = e(nc.psum_tensor("PF", [128, 512], F32))

    PQ, PK, PV, PW = 0, 2048, 4096, 6176
    PN2, PM1S, PM2S = 0, 8192, 10240
    PWO, POF = 0, 8192

    C_SUM, C_SQ, C_M, C_R, C_RM = 0, 16, 32, 48, 64
    C_U, C_U23 = 80, 81
    C_C1, C_T1, C_NT1 = 82, 83, 84
    C_C2, C_T2, C_NT2 = 85, 86, 87
    C_W = 96
    C_RD = 112
    C_S1 = 120
    C_Y = 136
    C_X = 144

    S_LN = 0
    S_WST = 1024
    S_WOS = 1024
    S_QP = 1024
    S_HSC = 1024
    S_UB = 1560
    S_BC = 2100

    RMF = SCR[0:64, 0:TBW]
    X2RB = SCR[:, 2700:2700 + D]

    em = Emitter()
    E = em.emit
    sem_names = ("lc", "lx", "lw", "lq", "lm", "lf", "lr", "ls",
                 "sv", "sa", "st", "sg", "so")
    for name in sem_names:
        em.add_sem(name, e(nc.semaphore(name)))

    for (dst, src) in ((QKB, qkcols), (CW, cwcols), (CBc, cbcols), (CC, ccols),
                       (ID32, cid32), (ONER, conerow), (BROW, bvrow),
                       (IDBF, cidbf), (DIAG, cdiag)):
        E("sync", lambda s, dst=dst, src=src: s.dma_start(out=dst[:], in_=src),
          inc=("lc", 16))
    lc_all = em.now("lc")
    for tt in range(NT):
        E("sync", lambda s, tt=tt: s.dma_start(
            out=X[:, tt * D:(tt + 1) * D], in_=xin[tt * 128:(tt + 1) * 128, :]),
          inc=("lx", 16))
    lx_all = em.now("lx")
    wst = SCR[:, S_WST:S_WST + 3072]
    E("sync", lambda s: s.dma_start(out=wst, in_=wqkv[0]), inc=("lq", 16))

    E("vector", lambda v: v.tensor_scalar(
        COL[:, C_U:C_U + 1], CC[:, 2:3], 0.5, -8388608.0, OP.add, OP.add),
      waits=[("lc", lc_all)], inc=("sv", 1))
    E("vector", lambda v: v.tensor_single_scalar(
        COL[:, C_U23:C_U23 + 1], COL[:, C_U:C_U + 1], MAGIC, OP.add),
      waits=[("sv", em.now("sv"))], inc=("sv", 1))
    E("vector", lambda v: v.memset(
        PAIR.ap()[:, PV + 64:PV + 16 * 130:130], 1.0), inc=("sv", 1))
    E("vector", lambda v: v.memset(
        PAIR.ap()[:, PV + 129:PV + 16 * 130:130], 1.0), inc=("sv", 1))
    E("vector", lambda v: v.memset(S2R[0:1, 0:1], 1.0), inc=("sv", 1))
    E("vector", lambda v: v.memset(S2R[0:1, S + 1:S + 2], 1.0), inc=("sv", 1))
    E("vector", lambda v: v.memset(COL[:, C_S1:C_S1 + 16], 1.0), inc=("sv", 1))
    E("vector", lambda v: v.tensor_single_scalar(
        QKB[:, 0:16:2], QKB[:, 0:16:2], 0.125, OP.mult),
      waits=[("lc", lc_all)], inc=("sv", 1))
    sv_init = em.now("sv")

    def dump_and_stop(bfap, f32ap, waits):
        if bfap is not None:
            E("sync", lambda s: s.dma_start(out=dbg[:, 0:bfap.shape[1]], in_=bfap),
              waits=waits, inc=("so", 16))
        if f32ap is not None:
            E("sync", lambda s: s.dma_start(out=dbgf[:, 0:f32ap.shape[1]], in_=f32ap),
              waits=waits, inc=("so", 16))
        E("gpsimd", None, waits=[("so", em.now("so"))])
        with nc.Block() as block:
            em.replay(block)
        ctx.close()
        return nc

    def ln_stats(tt, waits):
        xt = X[:, tt * D:(tt + 1) * D]
        E("scalar", lambda a, xt=xt, tt=tt: a.activation(
            xt, xt, AF.Copy,
            accum_out=COL[:, C_SUM + tt:C_SUM + tt + 1]),
          waits=waits + [("sa", em.now("sa")), ("sv", em.now("sv"))],
          inc=("sa", 1))
        E("scalar", lambda a, tt=tt: a.activation(
            SCR[:, S_LN:S_LN + D], X[:, tt * D:(tt + 1) * D], AF.Square,
            accum_out=COL[:, C_SQ + tt:C_SQ + tt + 1]),
          inc=("sa", 1))
        return em.now("sa")

    def ln_cols(tt, sa_wait):
        mc = COL[:, C_M + tt:C_M + tt + 1]
        rc = COL[:, C_R + tt:C_R + tt + 1]
        vw = COL[:, C_W:C_W + 1]
        t2 = COL[:, C_W + 1:C_W + 2]
        E("vector", lambda v, mc=mc, tt=tt: v.tensor_single_scalar(
            mc, COL[:, C_SUM + tt:C_SUM + tt + 1], 1.0 / D, OP.mult),
          waits=[("sa", sa_wait), ("sv", em.now("sv"))], inc=("sv", 1))
        E("vector", lambda v, vw=vw, mc=mc: v.tensor_tensor(vw, mc, mc, OP.mult),
          inc=("sv", 1))
        E("vector", lambda v, vw=vw, tt=tt: v.scalar_tensor_tensor(
            vw, COL[:, C_SQ + tt:C_SQ + tt + 1], 1.0 / D, vw, OP.mult, OP.subtract),
          inc=("sv", 1))
        E("vector", lambda v, vw=vw: v.tensor_single_scalar(vw, vw, 1e-5, OP.add),
          inc=("sv", 1))
        sv_v = em.now("sv")
        E("scalar", lambda a, rc=rc, vw=vw: a.activation(rc, vw, AF.Ln),
          waits=[("sv", sv_v), ("sa", em.now("sa"))], inc=("sa", 1))
        E("scalar", lambda a, rc=rc: a.activation(rc, rc, AF.Exp, scale=-0.5),
          inc=("sa", 1))
        sa_z = em.now("sa")
        for _ in range(2):
            E("vector", lambda v, t2=t2, rc=rc: v.tensor_tensor(t2, rc, rc, OP.mult),
              waits=[("sa", sa_z), ("sv", em.now("sv"))], inc=("sv", 1))
            E("vector", lambda v, t2=t2, vw=vw: v.tensor_tensor(t2, t2, vw, OP.mult),
              inc=("sv", 1))
            E("vector", lambda v, t2=t2: v.tensor_scalar(
                t2, t2, -0.5, 1.5, OP.mult, OP.add), inc=("sv", 1))
            E("vector", lambda v, rc=rc, t2=t2: v.tensor_tensor(rc, rc, t2, OP.mult),
              inc=("sv", 1))
        return em.now("sv")

    pt_sv = [0]

    def transpose_to(src_ap, dst_ap, slot, waits=()):
        p = PT[:, slot * 128:(slot + 1) * 128]
        E("tensor", lambda t, p=p, src_ap=src_ap: t.transpose(p, src_ap, IDBF[:]),
          waits=list(waits) + [("sv", pt_sv[0])], inc=("st", 1))
        stv = em.now("st")
        E("vector", lambda v, p=p, dst_ap=dst_ap: v.tensor_copy(dst_ap, p),
          waits=[("st", stv)], inc=("sv", 1))
        pt_sv[0] = em.now("sv")
        return em.now("sv")

    sa_s = [0] * NT
    for tt in range(NT):
        sa_s[tt] = ln_stats(tt, [("lx", 16 * (tt + 1)), ("lc", lc_all)])
    for tt in range(NT):
        ln_cols(tt, sa_s[tt])
    sv_ln1c = em.now("sv")
    for tt in range(NT):
        E("vector", lambda v, tt=tt: v.tensor_scalar(
            N1T[:, 0:D], X[:, tt * D:(tt + 1) * D],
            COL[:, C_M + tt:C_M + tt + 1], COL[:, C_R + tt:C_R + tt + 1],
            OP.subtract, OP.mult),
          waits=[("sv", sv_ln1c), ("st", em.now("st"))], inc=("sv", 1))
        sv_ap = em.now("sv")
        for dc in range(DC):
            transpose_to(N1T[:, dc * 128:(dc + 1) * 128],
                         B1[:, dc * 2048 + tt * 128:dc * 2048 + (tt + 1) * 128],
                         dc % 2, waits=[("sv", sv_ap)])
    sa_ln1fm = em.now("sa")
    sv_ln1fm = em.now("sv")

    if STAGE == "ln1":
        return dump_and_stop(B1[:, 0:8192], COL[:, 0:160],
                             [("sa", sa_ln1fm), ("sv", em.now("sv"))])

    wq_used = [0, 0]

    def meanabs_piece(wT, r, cl, acc_c, first):
        acc = COL[:, acc_c:acc_c + 1]
        tmp = COL[:, acc_c + 1:acc_c + 2]
        par = meanabs_piece.n % 2
        meanabs_piece.n += 1
        buf = WQ[:, par * 512:(par + 1) * 512]
        E("sync", lambda s: s.dma_start(
            out=buf, in_=wT[r * 128:(r + 1) * 128, cl * 512:(cl + 1) * 512]),
          waits=[("sv", wq_used[par])], inc=("lm", 16))
        lmv = em.now("lm")
        E("vector", lambda v: v.reduce_sum(
            tmp, buf, AX, apply_absolute_value=True),
          waits=[("lm", lmv)], inc=("sv", 1))
        if first:
            E("vector", lambda v: v.tensor_copy(acc, tmp), inc=("sv", 1))
        else:
            E("vector", lambda v: v.tensor_tensor(acc, acc, tmp, OP.add),
              inc=("sv", 1))
        wq_used[par] = em.now("sv")
    meanabs_piece.n = 0

    def meanabs_finish(nrows, ncols, acc_c, c_c):
        acc = COL[:, acc_c:acc_c + 1]
        sv_acc = em.now("sv")
        E("tensor", lambda t, acc=acc: t.matmul(
            PF[0:1, 0:1], acc, CC[:, 3:4], start=True, stop=True),
          waits=[("sv", sv_acc), ("sa", em.now("sa")), ("st", em.now("st"))],
          inc=("st", 1))
        E("scalar", lambda a, nrows=nrows, ncols=ncols: a.activation(
            COL[0:1, C_W + 4:C_W + 5], PF[0:1, 0:1], AF.Copy,
            scale=1.0 / (nrows * ncols)),
          waits=[("st", em.now("st"))], inc=("sa", 1))
        E("vector", lambda v: v.tensor_single_scalar(
            COL[0:1, C_W + 4:C_W + 5], COL[0:1, C_W + 4:C_W + 5], 1e-5, OP.max),
          waits=[("sa", em.now("sa")), ("sv", em.now("sv"))], inc=("sv", 1))
        E("tensor", lambda t: t.matmul(
            PF[:, 1:2], ONER[:], COL[0:1, C_W + 4:C_W + 5], start=True, stop=True),
          waits=[("sv", em.now("sv"))], inc=("st", 1))
        E("scalar", lambda a, c_c=c_c: a.copy(COL[:, c_c:c_c + 1], PF[:, 1:2]),
          waits=[("st", em.now("st")), ("sa", em.now("sa"))], inc=("sa", 1))
        sa_c = em.now("sa")
        E("vector", lambda v, c_c=c_c: v.tensor_single_scalar(
            COL[:, c_c + 1:c_c + 2], COL[:, c_c:c_c + 1], 0.5, OP.mult),
          waits=[("sa", sa_c), ("sv", em.now("sv"))], inc=("sv", 1))
        E("vector", lambda v, c_c=c_c: v.tensor_single_scalar(
            COL[:, c_c + 2:c_c + 3], COL[:, c_c:c_c + 1], -0.5, OP.mult),
          inc=("sv", 1))
        return em.now("sv")


    sa_po_evac = [0, 0]
    sa_ps = [0] * 4
    st_ps = [0] * 4
    et_use = [0] * 4
    st_pair_w = 0
    st_wbf_free = 0
    st_pv_prev = 0
    wbf = PAIR[:, PW:PW + 3072]

    E("vector", lambda v: v.tensor_copy(wbf, wst),
      waits=[("lq", 16), ("sv", sv_init)], inc=("sv", 1))
    sv_wbf_next = em.now("sv")
    E("sync", lambda s: s.dma_start(out=wst, in_=wqkv[1]),
      waits=[("sv", sv_wbf_next)], inc=("lq", 16))
    for c in range(NPAIR):
        sv_wbf = sv_wbf_next
        for which in range(2):
            coloff = 128 * which
            dstoff = PQ if which == 0 else PK
            for tb4 in range(4):
                bi = 2 * which + (tb4 % 2)
                bank = PS[bi]
                for dc in range(DC):
                    E("tensor", lambda t, dc=dc, coloff=coloff, tb4=tb4, bank=bank:
                      t.matmul(
                        bank[:], wbf[:, dc * 384 + coloff:dc * 384 + coloff + 128],
                        B1[:, dc * 2048 + tb4 * 512:dc * 2048 + (tb4 + 1) * 512],
                        start=(dc == 0), stop=(dc == DC - 1)),
                      waits=([("sv", sv_wbf), ("sv", sv_ln1fm),
                              ("sa", sa_ps[bi])] if dc == 0 else []),
                      inc=(("st", 1) if dc == DC - 1 else None))
                st_ps[bi] = em.now("st")
                E("scalar", lambda a, which=which, tb4=tb4, c=c, dstoff=dstoff,
                  bank=bank: a.activation(
                    PAIR[:, dstoff + tb4 * 512:dstoff + (tb4 + 1) * 512],
                    bank[:], AF.Identity,
                    bias=QKB[:, 2 * c + which:2 * c + which + 1],
                    scale=(0.125 if which == 0 else 1.0)),
                  waits=[("st", st_ps[bi])], inc=("sa", 1))
                sa_ps[bi] = em.now("sa")
        for tt in range(NT):
            bi = tt % 2
            bank = PS[bi]
            E("tensor", lambda t, c=c, bank=bank: t.matmul(
                bank[:, 0:128], ONER[:], BROW[:, c * 128:(c + 1) * 128],
                start=True, stop=False),
              waits=[("sa", sa_ps[bi]), ("st", st_pv_prev)])
            for dc in range(DC):
                E("tensor", lambda t, dc=dc, tt=tt, bank=bank: t.matmul(
                    bank[:, 0:128],
                    B1[:, dc * 2048 + tt * 128:dc * 2048 + (tt + 1) * 128],
                    wbf[:, dc * 384 + 256:dc * 384 + 384],
                    start=False, stop=(dc == DC - 1)),
                  inc=(("st", 1) if dc == DC - 1 else None))
            st_ps[bi] = em.now("st")
            E("scalar", lambda a, tt=tt, bank=bank: a.copy(
                PAIR[:, PV + tt * 130:PV + tt * 130 + 64], bank[:, 0:64]),
              waits=[("st", st_ps[bi]), ("sv", sv_init)], inc=("sa", 1))
            E("scalar", lambda a, tt=tt, bank=bank: a.copy(
                PAIR[:, PV + tt * 130 + 65:PV + tt * 130 + 129], bank[:, 64:128]),
              inc=("sa", 1))
            sa_ps[bi] = em.now("sa")
        sa_v = em.now("sa")
        st_wbf_free = em.now("st")
        if c + 1 < NPAIR:
            E("vector", lambda v: v.tensor_copy(wbf, wst),
              waits=[("lq", 16 * (c + 2)), ("st", st_wbf_free)],
              inc=("sv", 1))
            sv_wbf_next = em.now("sv")
            if c + 2 < NPAIR:
                E("sync", lambda s, c=c: s.dma_start(out=wst, in_=wqkv[c + 2]),
                  waits=[("sv", sv_wbf_next)], inc=("lq", 16))

        if STAGE == "qkv0" and c == 0:
            return dump_and_stop(PAIR[:, 0:8192], None,
                                 [("sa", sa_v), ("sv", em.now("sv"))])

        for qb in range(4):
            po_started = [False, False]
            for qi in range(qb * 4, qb * 4 + 4):
                for ki in ([qi - 1] if qi else []) + [qi]:
                    delta = qi - ki
                    for h2 in range(2):
                        bi = h2
                        bank = PS[bi]
                        E("tensor", lambda t, h2=h2, ki=ki, qi=qi, bank=bank:
                          t.matmul(
                            bank[:, 0:128],
                            PAIR[64 * h2:64 * h2 + 64, PK + ki * 128:PK + ki * 128 + 128],
                            PAIR[64 * h2:64 * h2 + 64, PQ + qi * 128:PQ + qi * 128 + 128],
                            start=True, stop=True),
                          waits=[("sa", sa_v), ("sa", sa_ps[bi])],
                          inc=("st", 1))
                        st_ps[bi] = em.now("st")
                        slot = 2 * h2 + (ki % 2)
                        et = ETB[:, slot * 128:(slot + 1) * 128]
                        E("scalar", lambda a, et=et, bank=bank, delta=delta:
                          a.activation(et, bank[:, 0:128], AF.Exp,
                                       bias=CC[:, delta:delta + 1]),
                          waits=[("st", st_ps[bi]), ("st", et_use[slot])],
                          inc=("sa", 1))
                        sa_ps[bi] = em.now("sa")
                        sa_exp = em.now("sa")
                        if delta == 0:
                            E("vector", lambda v, et=et: v.tensor_tensor(
                                et, et, DIAG[:], OP.mult),
                              waits=[("sa", sa_exp), ("sv", em.now("sv"))],
                              inc=("sv", 1))
                            dep = [("sv", em.now("sv"))]
                        else:
                            dep = [("sa", sa_exp)]
                        po = PO[h2]
                        first = not po_started[h2]
                        po_started[h2] = True
                        last = (qi == qb * 4 + 3) and (ki == qi)
                        E("tensor", lambda t, po=po, qi=qi, ki=ki, h2=h2, et=et,
                          first=first, last=last: t.matmul(
                            po[:, (qi % 4) * 65:(qi % 4) * 65 + 65], et,
                            PAIR[:, PV + ki * 130 + 65 * h2:PV + ki * 130 + 65 * h2 + 65],
                            start=first, stop=last),
                          waits=dep + [("sa", sa_po_evac[h2])],
                          inc=(("st", 1) if last else None))
                        et_use[slot] = em.now("st")
            st_po = em.now("st")
            st_pair_w = st_po
            st_pv_prev = st_po
            for h2 in range(2):
                po = PO[h2]
                E("vector", lambda v, po=po, h2=h2: v.reciprocal(
                    COL[:, C_RD + 4 * h2:C_RD + 4 * h2 + 4], po.ap()[:, 64:260:65]),
                  waits=[("st", st_po), ("sv", em.now("sv"))], inc=("sv", 1))
                sv_rd = em.now("sv")
                for s4 in range(4):
                    qi = qb * 4 + s4
                    hg = 2 * c + h2
                    E("scalar", lambda a, po=po, s4=s4, qi=qi, hg=hg, h2=h2:
                      a.activation(
                        B2[:, qi * 1024 + 64 * hg:qi * 1024 + 64 * hg + 64],
                        po[:, s4 * 65:s4 * 65 + 64], AF.Copy,
                        scale=COL[:, C_RD + 4 * h2 + s4:C_RD + 4 * h2 + s4 + 1]),
                      waits=[("sv", sv_rd), ("sa", em.now("sa"))], inc=("sa", 1))
                sa_po_evac[h2] = em.now("sa")
        for i in range(16):
            k = c * 16 + i
            if k < 64:
                meanabs_piece(w1T, k // 8, k % 8, C_W + 6, k == 0)
            else:
                k -= 64
                meanabs_piece(w2T, k // 2, k % 2, C_W + 8, k == 0)
    sa_attn = em.now("sa")

    if STAGE == "att0":
        return dump_and_stop(B2[:, 0:8192], COL[:, 0:160],
                             [("sa", sa_attn), ("sv", em.now("sv"))])

    h_done_st = [0, 0, 0, 0]
    sa_n1 = [0] * (NT + 1)
    sv_n1 = [0] * (NT + 1)

    def n1_tile(tt):
        w_main = tt // 2
        slot = w_main % 4
        xt = X[:, tt * D:(tt + 1) * D]
        scr = SCR[:, S_LN:S_LN + D]
        sa_st2 = ln_stats(tt, [("sv", sv_x2[tt])])
        ln_cols(tt, sa_st2)
        E("vector", lambda v, scr=scr, xt=xt, tt=tt: v.tensor_scalar(
            scr, xt, COL[:, C_M + tt:C_M + tt + 1], COL[:, C_R + tt:C_R + tt + 1],
            OP.subtract, OP.mult),
          waits=[("sv", em.now("sv"))], inc=("sv", 1))
        rmc = COL[:, C_RM + tt:C_RM + tt + 1]
        s1c = COL[:, C_S1 + tt:C_S1 + tt + 1]
        E("vector", lambda v, rmc=rmc, scr=scr: v.reduce_max(
            rmc, scr, AX, apply_absolute_value=True), inc=("sv", 1))
        E("vector", lambda v, rmc=rmc: v.tensor_single_scalar(
            rmc, rmc, 1e-5, OP.max), inc=("sv", 1))
        E("vector", lambda v, s1c=s1c, rmc=rmc: v.reciprocal(s1c, rmc),
          inc=("sv", 1))
        E("vector", lambda v, s1c=s1c: v.tensor_single_scalar(
            s1c, s1c, 127.0, OP.mult), inc=("sv", 1))
        for hh in range(4):
            E("vector", lambda v, scr=scr, s1c=s1c, hh=hh: v.tensor_scalar(
                WI[:, 0:256], scr[:, hh * 256:(hh + 1) * 256], s1c,
                COL[:, C_U23:C_U23 + 1], OP.mult, OP.add),
              waits=[("sa", em.now("sa")), ("sv", em.now("sv"))], inc=("sv", 1))
            sv_wi = em.now("sv")
            E("scalar", lambda a, hh=hh: a.activation(
                N1T[:, hh * 256:(hh + 1) * 256], WI[:, 0:256], AF.Copy,
                bias=-MAGIC),
              waits=[("sv", sv_wi), ("st", em.now("st")), ("sa", em.now("sa"))],
              inc=("sa", 1))
        win = N1B[:, slot * 2064:slot * 2064 + 2064]
        sa_fin = em.now("sa")
        for fc in range(DC):
            p = PT[:, (fc % 2) * 128:(fc % 2) * 128 + 128]
            E("tensor", lambda t, p=p, fc=fc: t.transpose(
                p, N1T[:, fc * 128:(fc + 1) * 128], IDBF[:]),
              waits=[("sa", sa_fin), ("sv", pt_sv[0]),
                     ("st", h_done_st[slot])],
              inc=("st", 1))
            stv = em.now("st")
            dst0 = fc * 258 + 1 + (tt % 2) * 128
            E("vector", lambda v, p=p, win=win, dst0=dst0: v.tensor_copy(
                win[:, dst0:dst0 + 128], p),
              waits=[("st", stv)], inc=("sv", 1))
            if tt % 2 == 0 and w_main >= 1:
                prv = N1B[:, ((w_main - 1) % 4) * 2064:((w_main - 1) % 4) * 2064 + 2064]
                E("vector", lambda v, p=p, prv=prv, fc=fc: v.tensor_copy(
                    prv[:, fc * 258 + 257:fc * 258 + 258], p[:, 0:1]),
                  inc=("sv", 1))
            pt_sv[0] = em.now("sv")
        if tt == 0:
            for fc in range(DC):
                E("scalar", lambda a, win=win, fc=fc: a.copy(
                    win[:, fc * 258:fc * 258 + 1], CC[:, 4:5]), inc=("sa", 1))
        if tt == NT - 1:
            for fc in range(DC):
                E("scalar", lambda a, win=win, fc=fc: a.copy(
                    win[:, fc * 258 + 257:fc * 258 + 258], CC[:, 4:5]),
                  inc=("sa", 1))
        E("tensor", lambda t, tt=tt: t.transpose(
            PF[0:128, 0:1].cast(F32) if False else PF[0:1, 0:128],
            COL[:, C_S1 + tt:C_S1 + tt + 1], ID32[:]),
          waits=[("sa", em.now("sa")), ("sv", em.now("sv"))], inc=("st", 1)) \
            if False else None
        sa_n1[tt] = em.now("sa")
        sv_n1[tt] = em.now("sv")


    meanabs_finish(D, CH, C_W + 6, C_C1)
    sv_c1 = em.now("sv")
    meanabs_finish(CH, D, C_W + 8, C_C2)
    sv_c2 = em.now("sv")

    pos = SCR[:, S_QP:S_QP + 512]

    def wquant_piece(wT, r, cl, c_c, sv_c, store):
        par = wquant_piece.n % 2
        wquant_piece.n += 1
        buf = WQ[:, par * 512:(par + 1) * 512]
        E("sync", lambda s: s.dma_start(
            out=buf, in_=wT[r * 128:(r + 1) * 128, cl * 512:(cl + 1) * 512]),
          waits=[("sv", wq_used[par])], inc=("lm", 16))
        lmv = em.now("lm")
        E("vector", lambda v: v.tensor_single_scalar(
            pos, buf, COL[:, c_c + 1:c_c + 2], OP.is_gt),
          waits=[("lm", lmv), ("sv", sv_c)], inc=("sv", 1))
        E("vector", lambda v: v.scalar_tensor_tensor(
            pos, buf, COL[:, c_c + 2:c_c + 3], pos, OP.is_lt, OP.subtract),
          inc=("sv", 1))
        wq_used[par] = em.now("sv")
        store(r, cl)
    wquant_piece.n = 0

    def store_m1(r, cl):
        col0 = cl * 512
        if col0 + 512 <= 2048:
            E("vector", lambda v: v.tensor_copy(
                B1[:, r * 2048 + col0:r * 2048 + col0 + 512], pos),
              waits=[("st", em.now("st"))], inc=("sv", 1))
        else:
            E("vector", lambda v: v.tensor_copy(ETB[:, 0:512], pos),
              waits=[("lf", em.now("lf")), ("st", em.now("st"))],
              inc=("sv", 1))
            svv = em.now("sv")
            for i in range(4):
                cb = (col0 - 2048) // 128 + i
                E("scalar", lambda a, cb=cb, i=i, r=r: a.dma_start(
                    out=m1h[cb, :, r, :],
                    in_=ETB[:, i * 128:(i + 1) * 128]),
                  waits=[("sv", svv)], inc=("lf", 16))

    def store_m2(r, cl):
        cb, j0 = r, cl * 512
        if cb < 16:
            E("vector", lambda v: v.tensor_copy(
                B2[:, cb * 1024 + j0:cb * 1024 + j0 + 512], pos),
              waits=[("st", em.now("st"))], inc=("sv", 1))
        else:
            E("vector", lambda v, j0=j0: v.tensor_copy(
                PAIR[:, PM2S + j0:PM2S + j0 + 512], pos),
              waits=[("lf", em.now("lf"))], inc=("sv", 1))
            svv = em.now("sv")
            if j0 == 512:
                E("scalar", lambda a, cb=cb: a.dma_start(
                    out=m2h[cb - 16, :, :], in_=PAIR[:, PM2S:PM2S + 1024]),
                  waits=[("sv", svv)], inc=("lf", 16))

    def s1_rows(tt0):
        E("tensor", lambda t: t.transpose(PF[0:16, 0:128], COL[:, C_S1:C_S1 + 16],
                                          ID32[:]),
          waits=[("sa", em.now("sa")), ("sv", em.now("sv")),
                 ("lr", em.now("lr"))],
          inc=("st", 1))
        stv = em.now("st")
        E("scalar", lambda a: a.copy(WQ[0:16, 0:128], PF[0:16, 0:128]),
          waits=[("st", stv)], inc=("sa", 1))
        sav = em.now("sa")
        for tt in (tt0, tt0 + 1):
            E("sync", lambda s, tt=tt: s.dma_start(
                out=S2R[0:1, 1 + tt * 128:1 + (tt + 1) * 128],
                in_=WQ[tt % 16:tt % 16 + 1, 0:128]),
              waits=[("sa", sav)], inc=("lr", 16))
        return ("lr", em.now("lr"))

    E("sync", lambda s: s.dma_start(out=BROW[:], in_=borow),
      waits=[("st", st_pair_w)], inc=("lw", 16))
    lw_bo = em.now("lw")
    for oc in range(DC):
        buf = SCR[:, S_WOS + (oc % 2) * 1024:S_WOS + (oc % 2) * 1024 + 1024]
        E("sync", lambda s, oc=oc, buf=buf: s.dma_start(
            out=buf, in_=woutT[oc * 128:(oc + 1) * 128, :]),
          waits=[("sv", em.now("sv"))], inc=("lw", 16))
        E("vector", lambda v, oc=oc, buf=buf: v.tensor_copy(
            PAIR[:, PWO + oc * 1024:PWO + (oc + 1) * 1024], buf),
          waits=[("lw", em.now("lw")), ("st", st_pair_w), ("sv", em.now("sv"))],
          inc=("sv", 1))
    sv_wo = em.now("sv")

    sv_x2 = [0] * NT
    sv_ps_free = [em.now("sv")] * 4
    of_done = [0, 0]
    st_of_used = [0, 0]

    def emit_of_transposes(tt):
        par = tt % 2
        dv = 0
        for oc in range(DC):
            dv = transpose_to(
                B2[:, tt * 1024 + oc * 128:tt * 1024 + (oc + 1) * 128],
                PAIR[:, POF + par * 1024 + oc * 128:POF + par * 1024 + (oc + 1) * 128],
                oc % 2,
                waits=[("sa", sa_attn), ("st", st_of_used[par])])
        of_done[par] = dv

    emit_of_transposes(0)
    for tt in range(NT):
        par = tt % 2
        if tt + 1 < NT:
            emit_of_transposes(tt + 1)
        for jb in range(2):
            bi = jb
            bank = PS[bi]
            E("tensor", lambda t, jb=jb, bank=bank: t.matmul(
                bank[:], ONER[:], BROW[:, jb * 512:(jb + 1) * 512],
                start=True, stop=False),
              waits=[("sv", of_done[par]), ("sv", sv_wo), ("sv", sv_ps_free[bi]),
                     ("lw", lw_bo)])
            for oc in range(DC):
                E("tensor", lambda t, oc=oc, jb=jb, bank=bank, par=par: t.matmul(
                    bank[:],
                    PAIR[:, POF + par * 1024 + oc * 128:POF + par * 1024 + (oc + 1) * 128],
                    PAIR[:, PWO + oc * 1024 + jb * 512:PWO + oc * 1024 + jb * 512 + 512],
                    start=False, stop=(oc == DC - 1)),
                  inc=(("st", 1) if oc == DC - 1 else None))
            if jb == 1:
                st_of_used[par] = em.now("st")
            st_ps[bi] = em.now("st")
            E("vector", lambda v, tt=tt, jb=jb, bank=bank: v.tensor_tensor(
                X[:, tt * D + jb * 512:tt * D + (jb + 1) * 512],
                X[:, tt * D + jb * 512:tt * D + (jb + 1) * 512],
                bank[:], OP.add),
              waits=[("st", st_ps[bi])], inc=("sv", 1))
            sv_ps_free[bi] = em.now("sv")
        sv_x2[tt] = em.now("sv")
        E("sync", lambda s, tt=tt: s.dma_start(
            out=out[tt * 128:(tt + 1) * 128, :], in_=X[:, tt * D:(tt + 1) * D]),
          waits=[("sv", sv_x2[tt])], inc=("so", 16))
        if tt < 8:
            n1_tile(tt)
        for i in range(4):
            k = tt * 4 + i
            wquant_piece(w1T, k // 8, k % 8, C_C1, sv_c1, store_m1)
        for cl in range(2):
            wquant_piece(w2T, tt, cl, C_C2, sv_c2, store_m2)
            wquant_piece(w2T, 16 + tt, cl, C_C2, sv_c2, store_m2)
    so_x2 = em.now("so")
    sv_outproj = em.now("sv")

    if STAGE == "x2":
        E("gpsimd", None, waits=[("so", em.now("so"))])
        with nc.Block() as block:
            em.replay(block)
        ctx.close()
        return nc


    sv_m1 = sv_m2 = em.now("sv")
    lf_wq = em.now("lf")

    sa_s1r = [("lr", 0)] * NTB
    for w in range(4):
        sa_s1r[w] = s1_rows(2 * w)

    sv_hs_free, sg_hs_free = em.now("sv"), em.now("sg")
    sv_w2_prev = em.now("sv")
    st_w2 = [em.now("st")] * 4
    st_m1buf = [em.now("st")] * 2
    st_m2buf = [em.now("st")] * 2
    st_ysc = 0
    sv_seed_ok = 0
    HS = 0
    sb1 = SCR[:, S_BC:S_BC + 258]
    sb2 = SCR[:, S_BC + 300:S_BC + 300 + TBW]
    lr_rb = 0

    col0_sa = [0] * (NTB + 1)
    for tb in range(NTB):
        slot = tb % 4
        win = N1B[:, slot * 2064:slot * 2064 + 2064]
        if tb + 1 < NTB:
            nslot = (tb + 1) % 4
            nwin = N1B[:, nslot * 2064:nslot * 2064 + 2064]
            E("scalar", lambda a, win=win, nwin=nwin: a.copy(
                nwin.rearrange("p (f c) -> p f c", f=8, c=258)[:, :, 0:1],
                win.rearrange("p (f c) -> p f c", f=8, c=258)[:, :, 256:257]),
              waits=[("sa", sa_n1[2 * tb + 1]), ("sv", sv_n1[2 * tb + 1]),
                     ("sa", sa_n1[min(2 * tb + 3, NT - 1)]),
                     ("sv", sv_n1[min(2 * tb + 3, NT - 1)]),
                     ("st", h_done_st[nslot]), ("sa", em.now("sa"))],
              inc=("sa", 1))
            col0_sa[tb + 1] = em.now("sa")
        win_ready = [("sa", sa_n1[min(2 * tb + 2, NT - 1)]),
                     ("sv", sv_n1[min(2 * tb + 2, NT - 1)]),
                     ("sa", sa_n1[2 * tb + 1]), ("sv", sv_n1[2 * tb + 1]),
                     ("sa", col0_sa[tb]), sa_s1r[tb]]
        E("tensor", lambda t, tb=tb: t.matmul(
            PF[:, 0:258], ONER[:], S2R[0:1, tb * TBW:tb * TBW + 258],
            start=True, stop=True),
          waits=win_ready + [("sa", em.now("sa")), ("sv", em.now("sv")),
                             ("st", em.now("st"))],
          inc=("st", 1))
        stv = em.now("st")
        E("vector", lambda v: v.reciprocal(sb1, PF[:, 0:258]),
          waits=[("st", stv), ("sv", em.now("sv"))], inc=("sv", 1))
        E("vector", lambda v: v.tensor_scalar(
            sb1, sb1, COL[:, C_C1:C_C1 + 1], -1.0, OP.mult, OP.mult),
          waits=[("sv", sv_m1)], inc=("sv", 1))
        sv_sb1 = em.now("sv")

        sv_run = 0
        sv_hsc = [0, 0]
        cb_order = [x for pr in zip(range(16), range(16, 32)) for x in pr]
        for cbi, cb in enumerate(cb_order):
            par = cbi % 2
            hbank = PO[par]
            if cb >= 16:
                mbuf = PAIR[:, PM1S + par * 1024:PM1S + par * 1024 + 1024]
                E("sync", lambda s, cb=cb, mbuf=mbuf: s.dma_start(
                    out=mbuf, in_=m1h[cb - 16].rearrange("p f c -> p (f c)")),
                  waits=[("lf", lf_wq), ("sv", sv_m1), ("st", st_m1buf[par])],
                  inc=("ls", 16))
                ls_m1 = em.now("ls")
            for fc in range(DC):
                if cb < 16:
                    lhsT = B1[:, fc * 2048 + cb * 128:fc * 2048 + (cb + 1) * 128]
                    lw = [("sv", sv_m1)] if fc == 0 else []
                else:
                    lhsT = PAIR[:, PM1S + par * 1024 + fc * 128:
                                PM1S + par * 1024 + (fc + 1) * 128]
                    lw = [("ls", ls_m1)] if fc == 0 else []
                E("tensor", lambda t, lhsT=lhsT, fc=fc, hbank=hbank, win=win:
                  t.matmul(hbank[:, 0:258], lhsT, win[:, fc * 258:(fc + 1) * 258],
                           start=(fc == 0), stop=(fc == DC - 1)),
                  waits=(lw + win_ready +
                         [("sv", sv_hsc[par]), ("sa", sa_po_evac[par])])
                  if fc == 0 else lw,
                  inc=(("st", 1) if fc == DC - 1 else None))
            st_h = em.now("st")
            if cb >= 16:
                st_m1buf[par] = st_h
            h_done_st[slot] = st_h
            hsc = SCR[:, S_HSC + par * 260:S_HSC + par * 260 + 258]
            E("vector", lambda v, hsc=hsc, hbank=hbank: v.tensor_tensor(
                hsc, hbank[:, 0:258], sb1, OP.mult),
              waits=[("st", st_h), ("sv", sv_sb1), ("sg", em.now("sg")),
                     ("sv", em.now("sv"))],
              inc=("sv", 1))
            sv_hsc[par] = em.now("sv")
            ub = SCR[:, S_UB + par * 260:S_UB + par * 260 + 256]
            E("vector", lambda v, ub=ub, hsc=hsc, cb=cb: v.tensor_single_scalar(
                ub, hsc[:, 0:256], CW[:, 3 * cb:3 * cb + 1], OP.mult),
              inc=("sv", 1))
            E("vector", lambda v, ub=ub, hsc=hsc, cb=cb: v.scalar_tensor_tensor(
                ub, hsc[:, 1:257], CW[:, 3 * cb + 1:3 * cb + 2], ub,
                OP.mult, OP.add),
              inc=("sv", 1))
            E("vector", lambda v, ub=ub, hsc=hsc, cb=cb: v.scalar_tensor_tensor(
                ub, hsc[:, 2:258], CW[:, 3 * cb + 2:3 * cb + 3], ub,
                OP.mult, OP.add),
              inc=("sv", 1))
            sv_conv = em.now("sv")
            hss = X[:, HS + cb * TBW:HS + (cb + 1) * TBW]
            E("scalar", lambda a, hss=hss, ub=ub, cb=cb: a.activation(
                hss, ub, AF.Silu, bias=CBc[:, cb:cb + 1]),
              waits=[("sv", sv_conv), ("sv", sv_hs_free), ("sa", em.now("sa"))],
              inc=("sa", 1))
            sa_silu = em.now("sa")
        hsview = X[:, 0:NCB * TBW].rearrange("p (cb t) -> p t cb", cb=NCB, t=TBW)
        E("vector", lambda v, hsview=hsview: v.reduce_max(
            RMX[:, 0:TBW], hsview, AX, apply_absolute_value=True),
          waits=[("sa", sa_silu), ("st", st_ysc), ("sv", em.now("sv"))],
          inc=("sv", 1))
        sv_run = em.now("sv")
        for half in range(2):
            E("tensor", lambda t, half=half: t.transpose(
                PF[:, half * 128:(half + 1) * 128],
                RMX[:, half * 128:(half + 1) * 128], ID32[:]),
              waits=[("sv", sv_run), ("st", em.now("st"))],
              inc=("st", 1))
        st_tr = em.now("st")
        rmc = COL[:, C_X:C_X + 2]
        for half in range(2):
            E("vector", lambda v, half=half: v.reduce_max(
                rmc[:, half:half + 1], PF[:, half * 128:(half + 1) * 128], AX),
              waits=[("st", st_tr)], inc=("sv", 1))
        E("vector", lambda v, rmc=rmc: v.tensor_single_scalar(
            rmc, rmc, 1e-5, OP.max), inc=("sv", 1))
        sv_rm = em.now("sv")
        ysc = COL[:, C_Y + (tb % 2) * 2:C_Y + (tb % 2) * 2 + 2]
        E("vector", lambda v, ysc=ysc, rmc=rmc: v.tensor_scalar(
            ysc, rmc, COL[:, C_C2:C_C2 + 1], -1.0 / 127.0, OP.mult, OP.mult),
          waits=[("sv", sv_m2)], inc=("sv", 1))
        sv_ysc = em.now("sv")
        s2c = COL[:, C_X + 2:C_X + 4]
        E("vector", lambda v, s2c=s2c, rmc=rmc: v.reciprocal(s2c, rmc),
          inc=("sv", 1))
        E("vector", lambda v, s2c=s2c: v.tensor_single_scalar(
            s2c, s2c, 127.0, OP.mult), inc=("sv", 1))
        sv_s2 = em.now("sv")
        for half in range(2):
            E("tensor", lambda t, half=half, s2c=s2c: t.matmul(
                PF[0:1, 256 + half * 128:256 + (half + 1) * 128],
                s2c[:, half:half + 1], ID32[:], start=True, stop=True),
              waits=[("sv", sv_s2)], inc=("st", 1))
        stv = em.now("st")
        row = WQ[0:1, 256:256 + TBW]
        E("scalar", lambda a, row=row: a.copy(row, PF[0:1, 256:512]),
          waits=[("st", stv)], inc=("sa", 1))
        sa_row2 = em.now("sa")
        E("tensor", lambda t, row=row: t.matmul(
            PF[:, 0:TBW], ONER[:], row, start=True, stop=True),
          waits=[("sa", sa_row2), ("sv", em.now("sv"))],
          inc=("st", 1))
        stv = em.now("st")
        st_ysc = stv
        E("scalar", lambda a: a.copy(sb2, PF[:, 0:TBW]),
          waits=[("st", stv)], inc=("sa", 1))
        sa_s2b = em.now("sa")
        sa_n2cb = [0] * NCB
        for cb in cb_order:
            hss = X[:, HS + cb * TBW:HS + (cb + 1) * TBW]
            E("vector", lambda v, hss=hss: v.tensor_tensor(hss, hss, sb2, OP.mult),
              waits=[("sa", sa_s2b), ("sv", em.now("sv"))], inc=("sv", 1))
            E("vector", lambda v, hss=hss: v.tensor_scalar(
                WI[:, 256:256 + TBW], hss, 1.0, COL[:, C_U23:C_U23 + 1],
                OP.mult, OP.add),
              waits=[("sa", em.now("sa"))], inc=("sv", 1))
            sv_wiq = em.now("sv")
            E("scalar", lambda a, cb=cb: a.activation(
                PAIR[:, PN2 + cb * TBW:PN2 + (cb + 1) * TBW], WI[:, 256:256 + TBW],
                AF.Copy, bias=-MAGIC),
              waits=[("sv", sv_wiq), ("sa", em.now("sa"))], inc=("sa", 1))
            sa_n2cb[cb] = em.now("sa")
        sa_n2 = em.now("sa")
        sv_hs_free = em.now("sv")

        if STAGE == "ffn0" and tb == DBG_TB:
            E("sync", lambda s: s.dma_start(out=dbgh[:, 0:8192], in_=X[:, 0:8192]),
              waits=[("sv", em.now("sv")), ("sa", em.now("sa"))], inc=("so", 16))
            E("sync", lambda s: s.dma_start(out=dbgf[0:1, 0:512], in_=S2R[0:1, 0:512]),
              waits=[("sv", em.now("sv"))], inc=("so", 16))
            E("sync", lambda s: s.dma_start(out=dbgf[1:2, 0:160], in_=COL[0:1, 0:160]),
              inc=("so", 16))
            E("sync", lambda s: s.dma_start(out=dbgf[2:128, 0:160],
                                            in_=COL[1:127, 0:160]),
              inc=("so", 16))
            E("sync", lambda s: s.dma_start(out=dbgf[0:128, 160:418],
                                            in_=SCR[:, 2100:2358]),
              inc=("so", 16))
            return dump_and_stop(N1B[:, (tb % 4) * 2064:(tb % 4) * 2064 + 2064], None,
                                 [("sa", sa_n2), ("sv", em.now("sv"))])

        for cbi, cb in enumerate(cb_order):
            if cb < 16:
                rhs = B2[:, cb * 1024:(cb + 1) * 1024]
                rw = [("sv", sv_m2)] if cb == 0 else []
            else:
                mbuf = PAIR[:, PM2S:PM2S + 1024]
                E("sync", lambda s, cb=cb, mbuf=mbuf: s.dma_start(
                    out=mbuf, in_=m2h[cb - 16]),
                  waits=[("lf", lf_wq), ("sv", sv_m2), ("st", st_m2buf[0])],
                  inc=("ls", 16))
                rhs = mbuf
                rw = [("ls", em.now("ls"))]
            for ht in range(2):
                for jb in range(2):
                    bank = PS[2 * ht + jb]
                    E("tensor", lambda t, cb=cb, ht=ht, jb=jb, bank=bank, rhs=rhs,
                      cbi=cbi:
                      t.matmul(
                        bank[:],
                        PAIR[:, PN2 + cb * TBW + ht * 128:PN2 + cb * TBW + ht * 128 + 128],
                        rhs[:, jb * 512:(jb + 1) * 512],
                        start=(cbi == 0), stop=(cbi == NCB - 1)),
                      waits=(rw + [("sa", sa_n2cb[cb]), ("sv", sv_w2_prev)])
                      if (ht == 0 and jb == 0) else [],
                      inc=(("st", 1) if (ht == 1 and jb == 1) else None))
            if cb >= 16:
                st_m2buf[0] = em.now("st")
        st_w2e = em.now("st")
        if STAGE == "w2d" and tb == DBG_TB:
            for bi in range(4):
                E("vector", lambda v, bi=bi: v.tensor_copy(
                    SCR[:, bi * 512:(bi + 1) * 512], PS[bi][:]),
                  waits=[("st", st_w2e), ("sv", em.now("sv"))], inc=("sv", 1))
            svv = em.now("sv")
            E("sync", lambda s: s.dma_start(out=dbgh[:, 0:2048],
                                            in_=SCR[:, 0:2048]),
              waits=[("sv", svv)], inc=("so", 16))
            E("sync", lambda s: s.dma_start(out=dbg[:, 0:8192],
                                            in_=PAIR[:, PN2:PN2 + 8192]),
              waits=[("sa", sa_n2)], inc=("so", 16))
            E("sync", lambda s: s.dma_start(out=dbgf[0:1, 0:160],
                                            in_=COL[0:1, 0:160]),
              inc=("so", 16))
            E("gpsimd", None, waits=[("so", em.now("so"))])
            with nc.Block() as block:
                em.replay(block)
            ctx.close()
            return nc
        for ht in range(2):
            tt = 2 * tb + ht
            E("sync", lambda s, tt=tt: s.dma_start(
                out=X2RB[:, 0:D], in_=out[tt * 128:(tt + 1) * 128, :]),
              waits=[("so", em.now("so")), ("sv", em.now("sv"))],
              inc=("lr", 16))
            lr_rb = em.now("lr")
            for jb in range(2):
                bank = PS[2 * ht + jb]
                ysc1 = COL[:, C_Y + (tb % 2) * 2 + ht:C_Y + (tb % 2) * 2 + ht + 1]
                E("vector", lambda v, bank=bank, jb=jb, ysc1=ysc1:
                  v.scalar_tensor_tensor(
                    X2RB[:, jb * 512:(jb + 1) * 512],
                    bank[:], ysc1,
                    X2RB[:, jb * 512:(jb + 1) * 512],
                    OP.mult, OP.add),
                  waits=[("st", st_w2e), ("lr", lr_rb), ("sv", sv_ysc),
                         ("sv", em.now("sv"))],
                  inc=("sv", 1))
            sv_y = em.now("sv")
            E("sync", lambda s, tt=tt: s.dma_start(
                out=out[tt * 128:(tt + 1) * 128, :], in_=X2RB[:, 0:D]),
              waits=[("sv", sv_y)], inc=("so", 16))
        sv_w2_prev = em.now("sv")

        if 2 * tb + 8 < NT:
            n1_tile(2 * tb + 8)
            n1_tile(2 * tb + 9)
            sa_s1r[tb + 4] = s1_rows(2 * tb + 8)

    E("gpsimd", None, waits=[("so", em.now("so"))])

    with nc.Block() as block:
        em.replay(block)
    ctx.close()
    return nc


_BF = ml_dtypes.bfloat16


def kernel(L0_attn_x, L0_attn_w_qkv, L0_attn_b_qkv, L0_attn_w_out, L0_attn_b_out,
           L0_ffn_w1, L0_ffn_conv_w, L0_ffn_conv_b, L0_ffn_w2):
    x = np.ascontiguousarray(np.asarray(L0_attn_x, dtype=np.float32))
    w_qkv = np.asarray(L0_attn_w_qkv, dtype=np.float32)
    b_qkv = np.asarray(L0_attn_b_qkv, dtype=np.float32)
    w_out = np.asarray(L0_attn_w_out, dtype=np.float32)
    b_out = np.asarray(L0_attn_b_out, dtype=np.float32)
    w1 = np.asarray(L0_ffn_w1, dtype=np.float32)
    conv_w = np.asarray(L0_ffn_conv_w, dtype=np.float32)
    conv_b = np.asarray(L0_ffn_conv_b, dtype=np.float32)
    w2 = np.asarray(L0_ffn_w2, dtype=np.float32)

    wqkvT = w_qkv.T
    wqkv_pair = np.ascontiguousarray(np.stack([
        np.concatenate([wqkvT[:, 128 * c:128 * c + 128],
                        wqkvT[:, 1024 + 128 * c:1024 + 128 * c + 128],
                        wqkvT[:, 2048 + 128 * c:2048 + 128 * c + 128]],
                       axis=1).reshape(8, 128, 384).transpose(1, 0, 2).reshape(128, 3072)
        for c in range(NPAIR)]))
    qkcols = np.zeros((128, 16), dtype=np.float32)
    for c in range(NPAIR):
        qkcols[:, 2 * c] = b_qkv[128 * c:128 * c + 128]
        qkcols[:, 2 * c + 1] = b_qkv[1024 + 128 * c:1024 + 128 * c + 128]
    bvrow = np.ascontiguousarray(b_qkv[2048:3072])[None, :]
    borow = np.ascontiguousarray(b_out)[None, :]
    woutT_a = np.ascontiguousarray(w_out.T)
    w1T_a = np.ascontiguousarray(w1.T)
    w2T_a = np.ascontiguousarray(w2.T)
    cw = conv_w.reshape(CH, 3)
    cwcols = np.zeros((128, 96), dtype=np.float32)
    cbcols = np.zeros((128, 32), dtype=np.float32)
    for cb in range(NCB):
        for k in range(3):
            cwcols[:, 3 * cb + k] = cw[cb * 128:(cb + 1) * 128, k]
        cbcols[:, cb] = conv_b[cb * 128:(cb + 1) * 128]
    kp = np.arange(128, dtype=np.float32)[:, None]
    qf = np.arange(128, dtype=np.float32)[None, :]
    ccols = np.zeros((128, 8), dtype=np.float32)
    ccols[:, 0:1] = kp - 64.0
    ccols[:, 1:2] = kp - 192.0
    ccols[:, 2] = 8388608.0
    ccols[:, 3] = 1.0
    cdiag = (kp <= qf).astype(_BF)
    cid32 = np.eye(128, dtype=np.float32)
    cidbf = np.eye(128, dtype=np.float32).astype(_BF)
    conerow = np.ones((1, 128), dtype=np.float32)

    nc = build_program()
    in_maps = [{
        "xin": np.ascontiguousarray(x[b]),
        "wqkv": wqkv_pair, "woutT": woutT_a, "w1T": w1T_a, "w2T": w2T_a,
        "qkcols": qkcols, "bvrow": bvrow, "borow": borow,
        "cwcols": cwcols, "cbcols": cbcols,
        "cid32": cid32, "cidbf": cidbf, "cdiag": cdiag, "ccols": ccols,
        "conerow": conerow,
    } for b in range(8)]
    res = run_bass_kernel_spmd(nc, in_maps, core_ids=list(range(8)))
    return np.stack([r["out"] for r in res.results])
